# version 11
# baseline (speedup 1.0000x reference)
import numpy as np
import concourse.bass as bass
import concourse.mybir as mybir
from concourse.bass_utils import run_bass_kernel_spmd
from contextlib import ExitStack

F32 = mybir.dt.float32
BF16 = mybir.dt.bfloat16
AF = mybir.ActivationFunctionType
ALU = mybir.AluOpType

P = 128
S = 4096
D = 1024
ND = D // P
BS = 512
NB = S // BS
FBS = 512
NFB = S // FBS
DH = 4096
NH = DH // P
NO1 = 2048 // P
NFT = 4096 // P
MAGIC = float(np.float32(1.5 * 2.0 ** 23))
TWO_PI = float(2.0 * np.pi)
INV_2PI = float(1.0 / (2.0 * np.pi))
EPS = 1e-5

ENGINES = ("sp", "act", "pe", "dve", "gp")
DBG = False
PARANOID = False
SKIP_FFN = False
SKIP_ATT = False
SKIP_PC = False
PC_PER_BLOCK = 18
PARANOID_SECTIONS = set()


class Plan:

    CHAINED = {"dve": "ch_dve", "act": "ch_act", "sp": "ch_sp"}

    class _MockIns:
        def then_inc(self, *a, **k):
            return self

    class _MockEng:

        def __init__(self):
            self.writes = []
            self.reads = []

        def __getattr__(self, name):
            rec_writes, rec_reads = self.writes, self.reads

            def rec(*args, **kwargs):
                outs, ins = [], []
                if "out" in kwargs:
                    outs.append(kwargs["out"])
                pos = [a for a in args if hasattr(a, "tensor")]
                if not outs and pos:
                    outs.append(pos[0])
                    pos = pos[1:]
                ins.extend(pos)
                for k, v in kwargs.items():
                    if k != "out" and hasattr(v, "tensor"):
                        ins.append(v)
                rec_writes.extend(a.tensor.name for a in outs)
                rec_reads.extend(a.tensor.name for a in ins)
                return Plan._MockIns()

            return rec

    QDEPTHS = {"sp": 20, "gpd": 28, "ac": 28}

    ENGINE_CHAINS = {
        "dve": ("ch_dve",),
        "act": ("ch_act",) + tuple(f"q_ac{j}" for j in range(QDEPTHS["ac"])),
        "sp": tuple(f"q_sp{j}" for j in range(QDEPTHS["sp"])),
        "gp": ("ch_gpc",) + tuple(f"q_gpd{j}" for j in range(QDEPTHS["gpd"])),
    }

    def __init__(self):
        self.ops = {e: [] for e in ENGINES}
        self.cnt = {}
        self.lmap = {}
        self.dep = {}
        self.qidx = {}
        self.section = "setup"

    def cur(self, sem):
        return self.cnt.get(sem, 0)

    def lag(self, sem, k):
        if PARANOID or self.section in PARANOID_SECTIONS:
            k = 0
        return max(0, self.cur(sem) - k)

    def _dma_queue(self, eng):
        return {"sp": "sp", "gp": "gpd", "act": "ac"}.get(eng)

    def _chain_of(self, eng, n):
        if n == 16:
            q = self._dma_queue(eng)
            i = self.qidx.get(q, 0)
            return f"q_{q}{i % self.QDEPTHS[q]}"
        if eng == "gp":
            return "ch_gpc"
        return self.CHAINED.get(eng)

    def barrier(self):
        snap = []
        for sem in ("ch_dve", "ch_act", "ch_gpc"):
            if self.cur(sem) > 0:
                snap.append((sem, self.cur(sem)))
        for q in ("sp", "gpd", "ac"):
            for j in range(self.QDEPTHS[q]):
                nm = f"q_{q}{j}"
                if self.cur(nm) > 0:
                    snap.append((nm, self.cur(nm)))
        for key, v in list(self.cnt.items()):
            if isinstance(key, tuple) and key[0] == "real" and v > 0:
                snap.append((key[1], v))
        self.pending_barrier = {e: list(snap) for e in ENGINES}

    def emit(self, eng, fn, waits=(), inc=None, n=1):
        w = []
        pb = getattr(self, "pending_barrier", None)
        if pb and pb.get(eng):
            for sem, val in pb.pop(eng):
                w.append((sem, val))
        for sem, val in waits:
            if val is None:
                val = self.cur(sem)
            if val <= 0:
                continue
            if sem in self.lmap:
                need = {}
                real = 0
                for lc, ch, va in self.lmap[sem]:
                    if ch is None:
                        real = va
                    else:
                        need[ch] = va
                    if lc >= val:
                        break
                else:
                    raise AssertionError(f"wait on future {sem} >= {val}")
                for ch, va in need.items():
                    w.append((ch, va))
                if real > 0:
                    w.append((sem, real))
            else:
                w.append((sem, val))
        chain = self._chain_of(eng, n)
        if chain is not None:
            if n == 16:
                q = self._dma_queue(eng)
                self.qidx[q] = self.qidx.get(q, 0) + 1
                prev = self.cur(chain)
                if prev > 0:
                    w.append((chain, prev))
            self.cnt[chain] = self.cur(chain) + n
            mock = Plan._MockEng()
            fn(mock)
            my_chains = self.ENGINE_CHAINS[eng]
            chain_after = self.cur(chain)
            need = {}
            for t in mock.reads:
                e = self.dep.get(t)
                if e and e["w"] and e["w"][0] in my_chains:
                    c, v = e["w"]
                    need[c] = max(need.get(c, 0), v)
            for t in mock.writes:
                e = self.dep.get(t)
                if e:
                    if e["w"] and e["w"][0] in my_chains:
                        c, v = e["w"]
                        need[c] = max(need.get(c, 0), v)
                    for c, v in e["r"].items():
                        if c in my_chains:
                            need[c] = max(need.get(c, 0), v)
            for c, v in need.items():
                if c == chain:
                    v = min(v, chain_after - n)
                if v > 0:
                    w.append((c, v))
            for t in mock.reads:
                e = self.dep.setdefault(t, {"w": None, "r": {}})
                e["r"][chain] = max(e["r"].get(chain, 0), chain_after)
            for t in mock.writes:
                e = self.dep.setdefault(t, {"w": None, "r": {}})
                e["w"] = (chain, chain_after)
        real_inc = None
        if inc is not None:
            self.cnt[inc] = self.cur(inc) + n
            ent = self.lmap.setdefault(inc, [])
            if chain is not None:
                ent.append((self.cur(inc), chain, self.cur(chain)))
            else:
                rc = self.cnt.get(("real", inc), 0) + n
                self.cnt[("real", inc)] = rc
                ent.append((self.cur(inc), None, rc))
                real_inc = inc
        self.ops[eng].append((fn, w, real_inc, n, chain))

    def replay(self, eng_name, eng, sems):
        waited = {}
        for fn, waits, inc, n, chain in self.ops[eng_name]:
            for sem, val in waits:
                if waited.get(sem, 0) < val:
                    eng.wait_ge(sems[sem], val)
                    waited[sem] = val
            ins = fn(eng)
            if ins is None:
                continue
            if chain is not None:
                ins.then_inc(sems[chain], n)
            elif inc is not None:
                ins.then_inc(sems[inc], n)


def build_nc():
    nc = bass.Bass("TRN2", target_bir_lowering=False)

    def din(name, shape, dtype=F32):
        return nc.dram_tensor(name, shape, dtype, kind="ExternalInput").ap()

    xT = din("xT", [D, S])
    womT = din("womT", [D, D])
    w1Ta = din("w1Ta", [4096, 2048])
    w2Ta = din("w2Ta", [2048, 1024])
    fw1T = din("fw1T", [D, DH])
    fw2T = din("fw2T", [D, DH])
    fh1T = din("fh1T", [DH, DH])
    fh2T = din("fh2T", [DH, DH])
    fw3T = din("fw3T", [DH, D])
    bom_c = din("bom_c", [P, ND])
    ls_c = din("ls_c", [P, ND])
    gam_c = din("gam_c", [P, NFT])
    bet_c = din("bet_c", [P, NFT])
    b1_c = din("b1_c", [P, NO1])
    b2_c = din("b2_c", [P, ND])
    s1_c = din("s1_c", [P, NH])
    s2_c = din("s2_c", [P, NH])
    hs1_c = din("hs1_c", [P, NH])
    hs2_c = din("hs2_c", [P, NH])
    s3_c = din("s3_c", [P, ND])
    sel2_in = din("sel2_in", [2, 2])
    posscale = din("posscale", [P, S])
    posinv = din("posinv", [P, S])

    outT = nc.dram_tensor("outT", [ND, P, S], F32, kind="ExternalOutput").ap()

    w1g_d = nc.dram_tensor("w1g_d", [NO1, P, NFT * P], BF16, kind="Internal").ap()
    wom_d = nc.dram_tensor("wom_d", [ND, P, ND * P], BF16, kind="Internal").ap()
    w2_d = nc.dram_tensor("w2_d", [ND, P, NO1 * P], BF16, kind="Internal").ap()
    fw1_d = nc.dram_tensor("fw1_d", [NH, P, ND * P], BF16, kind="Internal").ap()
    fw2_d = nc.dram_tensor("fw2_d", [NH, P, ND * P], BF16, kind="Internal").ap()
    fh1_d = nc.dram_tensor("fh1_d", [NH, P, NH * P], BF16, kind="Internal").ap()
    fh2_d = nc.dram_tensor("fh2_d", [NH, P, NH * P], BF16, kind="Internal").ap()
    fw3_d = nc.dram_tensor("fw3_d", [ND, P, NH * P], BF16, kind="Internal").ap()
    ctx_d = nc.dram_tensor("ctx_d", [NFT, P, S], BF16, kind="Internal").ap()
    x2f_d = nc.dram_tensor("x2f_d", [ND, P, S], F32,
                           kind="ExternalOutput" if DBG else "Internal").ap()
    ctxdbg = (nc.dram_tensor("ctxdbg", [NFT, P, S], BF16, kind="ExternalOutput").ap()
              if DBG else None)
    lndbg = (nc.dram_tensor("lndbg", [NB, 3, P, BS], F32, kind="ExternalOutput").ap()
             if DBG else None)
    folddbg = (nc.dram_tensor("folddbg", [P, 2 * NO1], F32,
                              kind="ExternalOutput").ap() if DBG else None)
    rowdbg = (nc.dram_tensor("rowdbg", [2, 2048], F32,
                             kind="ExternalOutput").ap() if DBG else None)
    combodbg = (nc.dram_tensor("combodbg", [P, 2 * NFT], BF16,
                               kind="ExternalOutput").ap() if DBG else None)
    bgdbg = (nc.dram_tensor("bgdbg", [P, NFT], F32,
                            kind="ExternalOutput").ap() if DBG else None)
    x2b_d = nc.dram_tensor("x2b_d", [ND, P, S], BF16, kind="Internal").ap()
    r3_d = nc.dram_tensor("r3_d", [ND, P, S], F32,
                          kind="ExternalOutput" if DBG else "Internal").ap()

    plan = Plan()
    sem_names = [
        "ch_dve", "ch_act", "ch_gpc",
        "ld_small", "ld_wom", "ld_womsl", "ld_w2", "els_done",
        "fold_ld", "fold_dve", "fold_st", "fold_pe", "fold_fin",
        "pc_ld", "pc_st",
        "axl", "axlb", "apos",
        "pe_om", "act_om", "dve_oms",
        "s_scp", "s_scr", "s_sci", "dve_cphi", "dve_cr", "dve_ci",
        "dve_trig", "act_trig", "dve_cm", "dve_q", "gp_uv", "act_sq",
        "pe_st1", "dve_var1", "act_sqrt1", "dve_rstd1",
        "sp_w1g", "pe_w1", "dve_zc", "gp_z", "act_h1a",
        "sp_ctxw", "sp_ctxst",
        "pe_w2", "act_rb", "dve_r", "act_sqr", "pe_st2", "axr", "gp_x2b",
        "dve_var2", "act_sqrt2", "dve_rstd2", "dve_x2", "sp_x2st",
        "f_x2l", "f_wl", "f_pe1", "f_dve1", "f_act1",
        "f_pe2", "f_act2", "f_pe3", "f_act3",
        "f_pe4", "f_x2fl", "f_dve4", "f_r3st", "f_sq3", "f_st3",
        "f_var3", "f_sqrt3", "f_rstd3", "f_r3l", "f_out", "f_outst", "dbg",
    ]

    sem_names += ([f"q_sp{j}" for j in range(20)] + [f"q_gpd{j}" for j in range(28)]
                  + [f"q_ac{j}" for j in range(28)])
    outer = ExitStack()
    with outer:
        class LazySems(dict):
            def __missing__(self, nm):
                h = outer.enter_context(nc.semaphore(nm))
                self[nm] = h
                return h
        sems = LazySems()

        def sbo(name, shape, dtype):
            return outer.enter_context(nc.sbuf_tensor(name, shape, dtype))

        cols = sbo("cols", [P, 8 * 16], F32)
        bom = cols[:, 0:8]
        lsc = cols[:, 16:24]
        els = cols[:, 32:40]
        bels = cols[:, 48:56]
        b2c = cols[:, 64:72]
        s3c = cols[:, 80:88]
        b1sb = sbo("b1sb", [P, NO1], F32)
        s1sb = sbo("s1sb", [P, NH], F32)
        s2sb = sbo("s2sb", [P, NH], F32)
        hs1sb = sbo("hs1sb", [P, NH], F32)
        hs2sb = sbo("hs2sb", [P, NH], F32)
        gamsb = sbo("gamsb", [P, NFT], F32)
        betsb = sbo("betsb", [P, NFT], F32)
        bgsb = sbo("bgsb", [P, NFT], F32)
        nw1s = sbo("nw1s", [P, NO1], F32)
        bias1 = sbo("bias1", [P, NO1], F32)
        onesb = sbo("onesb", [P, P], BF16)
        ones1 = sbo("ones1", [P, 4], F32)
        carr_p = sbo("carr_p", [P, ND], F32)
        carr_r = sbo("carr_r", [P, ND], F32)
        carr_i = sbo("carr_i", [P, ND], F32)
        sel2 = sbo("sel2", [2, 2], F32)

        def dma(dst, src):
            return lambda eng: eng.dma_start(out=dst, in_=src)

        for dst, src in [(bom, bom_c), (lsc, ls_c), (b2c, b2_c), (s3c, s3_c),
                         (b1sb[:], b1_c), (s1sb[:], s1_c), (s2sb[:], s2_c),
                         (hs1sb[:], hs1_c), (hs2sb[:], hs2_c),
                         (gamsb[:], gam_c), (betsb[:], bet_c),
                         (sel2[:], sel2_in)]:
            plan.emit("sp", dma(dst, src), inc="ld_small", n=16)
        N_SMALL = plan.cur("ld_small")

        plan.emit("act", lambda e: e.activation(els, lsc, AF.Exp),
                  waits=[("ld_small", N_SMALL)], inc="els_done")
        plan.emit("dve", lambda e: e.tensor_mul(bels, bom, els),
                  waits=[("els_done", 1)], inc="els_done")
        plan.emit("dve", lambda e: e.reciprocal(bgsb[:], gamsb[:]),
                  waits=[("ld_small", N_SMALL)], inc="els_done")
        plan.emit("dve", lambda e: e.tensor_mul(bgsb[:], bgsb[:], betsb[:]),
                  waits=[("els_done", None)], inc="els_done")
        plan.emit("dve", lambda e: e.memset(onesb[:], 1.0), inc="els_done")
        plan.emit("dve", lambda e: e.memset(ones1[:], 1.0), inc="els_done")
        ELS_ALL = plan.cur("els_done")

        sA = ExitStack()
        with sA:
            def sb(name, shape, dtype):
                return sA.enter_context(nc.sbuf_tensor(name, shape, dtype))

            def ps(name, shape):
                return sA.enter_context(nc.psum_tensor(name, shape, F32))

            plan.section = "fold"
            sF = ExitStack()
            with sF:
                w1t_sb = [sF.enter_context(nc.sbuf_tensor(f"w1t_sb{i}", [P, 2048], F32))
                          for i in range(2)]
                w1gf = [sF.enter_context(nc.sbuf_tensor(f"w1gf{i}", [P, 2048], BF16))
                        for i in range(2)]
                combo = sF.enter_context(nc.sbuf_tensor("combo", [P, 2 * NFT], BF16))
                rowtmp2 = sF.enter_context(nc.sbuf_tensor("rowtmp2", [2, 2048], F32))
                p_r12 = sF.enter_context(nc.psum_tensor("p_r12", [2, 2048], F32))
                p_cols = sF.enter_context(nc.psum_tensor("p_cols", [P, 2 * NO1], F32))

                plan.emit("dve", lambda e: e.memset(combo[:], 1.0),
                          waits=[("els_done", ELS_ALL)], inc=None)
                for ft in range(NFT):
                    plan.emit("dve", (lambda ft: lambda e: e.tensor_copy(
                        combo[:, 2 * ft + 1:2 * ft + 2],
                        bgsb[:, ft:ft + 1]))(ft),
                        inc="els_done" if ft == NFT - 1 else None)
                ELS_ALL2 = plan.cur("els_done")
                if DBG:
                    plan.emit("sp", lambda e: e.dma_start(
                        out=combodbg, in_=combo[:]),
                        waits=[("els_done", ELS_ALL2)], inc="dbg", n=16)
                    plan.emit("sp", lambda e: e.dma_start(
                        out=bgdbg, in_=bgsb[:]), inc="dbg", n=16)

                for ft in range(NFT):
                    tsb = w1t_sb[ft % 2]
                    gsb = w1gf[ft % 2]
                    plan.emit("sp", dma(tsb[:], w1Ta[ft * P:(ft + 1) * P, :]),
                              waits=[("fold_dve", plan.lag("fold_dve", 1))],
                              inc="fold_ld", n=16)
                    plan.emit("dve",
                              (lambda tsb, gsb, ft: lambda e: e.tensor_scalar_mul(
                                  gsb[:], tsb[:], gamsb[:, ft:ft + 1]))(tsb, gsb, ft),
                              waits=[("fold_ld", None), ("els_done", ELS_ALL2),
                                     ("fold_st", plan.lag("fold_st", 16)),
                                     ("fold_pe", plan.lag("fold_pe", 1))],
                              inc="fold_dve")
                    plan.emit("act",
                              (lambda gsb, ft: lambda e: e.dma_start(
                                  out=w1g_d[:, :, ft * P:(ft + 1) * P].rearrange(
                                      "o p c -> p o c"),
                                  in_=gsb[:].rearrange("p (o c) -> p o c", o=NO1)))(
                                  gsb, ft),
                              waits=[("fold_dve", None)], inc="fold_st", n=16)
                    for j in range(4):
                        plan.emit("pe",
                                  (lambda gsb, ft, j: lambda e: e.matmul(
                                      p_r12[:, j * 512:(j + 1) * 512],
                                      combo[:, 2 * ft:2 * ft + 2],
                                      gsb[:, j * 512:(j + 1) * 512],
                                      start=(ft == 0), stop=(ft == NFT - 1)))(
                                      gsb, ft, j),
                                  waits=[("fold_dve", None)],
                                  inc="fold_pe" if j == 3 else None)
                plan.emit("dve", lambda e: e.tensor_copy(rowtmp2[:], p_r12[:]),
                          waits=[("fold_pe", None)], inc="fold_fin")
                for o in range(NO1):
                    plan.emit("pe",
                              (lambda o: lambda e: e.matmul(
                                  p_cols[:, o:o + 1], rowtmp2[0:2, o * P:(o + 1) * P],
                                  sel2[0:2, 0:1], start=True, stop=True))(o),
                              waits=[("fold_fin", 1), ("ld_small", N_SMALL)],
                              inc=None)
                for o in range(NO1):
                    plan.emit("pe",
                              (lambda o: lambda e: e.matmul(
                                  p_cols[:, NO1 + o:NO1 + o + 1],
                                  rowtmp2[0:2, o * P:(o + 1) * P],
                                  sel2[0:2, 1:2], start=True, stop=True))(o),
                              inc="fold_fin" if o == NO1 - 1 else None)
                plan.emit("dve",
                          lambda e: e.tensor_scalar_mul(nw1s[:], p_cols[:, 0:NO1], -1.0),
                          waits=[("fold_fin", 2)], inc="fold_fin")
                plan.emit("dve",
                          lambda e: e.tensor_add(bias1[:], b1sb[:],
                                                 p_cols[:, NO1:2 * NO1]),
                          inc="fold_fin")
            plan.barrier()
            FOLD_DONE = plan.cur("fold_fin")
            if DBG:
                plan.emit("sp", lambda e: e.dma_start(
                    out=rowdbg, in_=rowtmp2[:]),
                    waits=[("fold_fin", FOLD_DONE)], inc="dbg", n=16)
                plan.emit("sp", lambda e: e.dma_start(
                    out=folddbg[:, 0:NO1], in_=nw1s[:]), inc="dbg", n=16)
                plan.emit("sp", lambda e: e.dma_start(
                    out=folddbg[:, NO1:2 * NO1], in_=bias1[:]),
                    inc="dbg", n=16)
            N_FOLD_ST = NFT * 16

            precast_jobs = []
            for dd in range(ND):
                precast_jobs.append((w2Ta[:, dd * P:(dd + 1) * P], w2_d[dd], 16))
            for og in range(NH):
                precast_jobs.append((fw1T[:, og * P:(og + 1) * P], fw1_d[og], ND))
            for og in range(NH):
                precast_jobs.append((fw2T[:, og * P:(og + 1) * P], fw2_d[og], ND))
            for o2 in range(NH):
                precast_jobs.append((fh1T[:, o2 * P:(o2 + 1) * P], fh1_d[o2], NH))
            for o2 in range(NH):
                precast_jobs.append((fh2T[:, o2 * P:(o2 + 1) * P], fh2_d[o2], NH))
            for dd in range(ND):
                precast_jobs.append((fw3T[:, dd * P:(dd + 1) * P], fw3_d[dd], NH))
            n_pcjobs = len(precast_jobs)
            pc_next = [0]

            def emit_precast(count):
                if SKIP_PC:
                    return
                for _ in range(count):
                    i = pc_next[0]
                    if i >= n_pcjobs:
                        return
                    pc_next[0] += 1
                    srcj, dst, kt = precast_jobs[i]
                    plan.emit("gp",
                              (lambda srcj, dst, kt: lambda e: e.dma_start(
                                  out=dst.rearrange("p (k c) -> p k c", k=kt),
                                  in_=srcj.rearrange("(k p) c -> p k c", p=P)))(
                                  srcj, dst, kt),
                              inc="pc_st", n=16)

            for j in range(ND):
                plan.emit("gp",
                          (lambda j: lambda e: e.dma_start(
                              out=wom_d[j].rearrange("p (k c) -> p k c", k=ND),
                              in_=womT[:, j * P:(j + 1) * P].rearrange(
                                  "(k p) c -> p k c", p=P)))(j),
                          waits=([("fold_fin", FOLD_DONE)] if j == 0 else []),
                          inc="ld_wom", n=16)
            N_WOM = plan.cur("ld_wom")
            womsl = [sb(f"womsl{i}", [P, ND * P], BF16) for i in range(2)]
            w2sl = [sb(f"w2sl{i}", [P, NO1 * P], BF16) for i in range(2)]

            xTf = sb("xTf", [P, ND * BS], F32)
            xTb = sb("xTb", [P, ND * BS], BF16)
            pos_s = sb("pos_s", [P, BS], F32)
            pos_i = sb("pos_i", [P, BS], F32)
            om = [sb(f"om{i}", [P, BS], F32) for i in range(2)]
            phi = [sb(f"phi{i}", [P, BS], F32) for i in range(2)]
            t_s = [sb(f"t_s{i}", [P, BS], F32) for i in range(2)]
            t_c = [sb(f"t_c{i}", [P, BS], F32) for i in range(2)]
            rn = sb("rn", [P, BS], F32)
            sinb = [sb(f"sin{i}", [P, BS], F32) for i in range(2)]
            cosb = [sb(f"cos{i}", [P, BS], F32) for i in range(2)]
            cm_r = sb("cm_r", [P, BS], F32)
            cm_i = sb("cm_i", [P, BS], F32)
            S_r = [sb(f"S_r{i}", [P, BS], F32) for i in range(2)]
            S_i = [sb(f"S_i{i}", [P, BS], F32) for i in range(2)]
            uv0 = [sb(f"uv0_{i}", [P, BS], F32) for i in range(2)]
            uv1 = [sb(f"uv1_{i}", [P, BS], F32) for i in range(2)]
            sqb = [sb(f"sqb{i}", [P, BS], BF16) for i in range(2)]
            qsta = [sb(f"qsta{i}", [P, 4 * BS], BF16) for i in range(2)]
            q_st = [[qsta[i][:, q * BS:(q + 1) * BS] for i in range(2)]
                    for q in range(4)]
            ctx_dq = ctx_d.rearrange("(q k) p s -> k q p s", k=ND)
            xres = [sb(f"xres{i}", [P, BS], F32) for i in range(2)]
            muBp = [sb(f"muB{i}", [P, BS], F32) for i in range(2)]
            veB = sb("veB", [P, BS], F32)
            rstdBp = [sb(f"rstdB{i}", [P, BS], F32) for i in range(2)]
            mu2B = sb("mu2B", [P, BS], F32)
            ve2B = sb("ve2B", [P, BS], F32)
            rstd2B = sb("rstd2B", [P, BS], F32)
            ctxw = sb("ctxw", [P, NFT * BS], BF16)
            w1g_sb = [sb(f"w1g_sb{i}", [P, 16 * P], BF16) for i in range(2)]
            z1 = sb("z1", [P, BS], F32)
            z2 = sb("z2", [P, BS], F32)
            h1a = sb("h1a", [P, NO1 * BS], BF16)
            rbuf = [sb(f"rbuf{i}", [P, BS], F32) for i in range(2)]
            r_sb = sb("r_sb", [P, ND * BS], F32)
            rbf = sb("rbf0", [P, BS], BF16)
            sqr = sb("sqr0", [P, BS], BF16)
            x2fs = [sb(f"x2fs{i}", [P, BS], F32) for i in range(2)]
            x2bs = [sb(f"x2bs{i}", [P, BS], BF16) for i in range(2)]

            p_om = [ps(f"p_om{i}", [P, BS]) for i in range(2)]
            p_s1 = ps("p_s1", [P, BS])
            p_s2 = ps("p_s2", [P, BS])
            p_z = [ps(f"p_z{i}", [P, BS]) for i in range(2)]
            p_t1 = ps("p_t1", [P, BS])
            p_t2 = ps("p_t2", [P, BS])

            needq = {}

            def front_pe(b, e8, AXLB):
                ws = womsl[e8 % 2]
                plan.emit("sp", (lambda e8, ws: lambda e: e.dma_start(
                    out=ws[:], in_=wom_d[e8]))(e8, ws),
                    waits=[("ld_wom", N_WOM),
                           ("pe_om", plan.lag("pe_om", 1))],
                    inc="ld_womsl", n=16)
                LDWS = plan.cur("ld_womsl")
                pom = p_om[e8 % 2]
                for dk in range(ND):
                    plan.emit("pe",
                              (lambda pom, dk, ws: lambda e: e.matmul(
                                  pom[:],
                                  ws[:, dk * P:(dk + 1) * P],
                                  xTb[:, dk * BS:(dk + 1) * BS],
                                  start=(dk == 0), stop=(dk == ND - 1)))(
                                  pom, dk, ws),
                              waits=[("axlb", AXLB), ("ld_womsl", LDWS),
                                     ("fold_fin", FOLD_DONE),
                                     ("act_om", plan.lag("act_om", 1))],
                              inc="pe_om" if dk == ND - 1 else None)

            def front_evac(b, e8):
                pom = p_om[e8 % 2]
                plan.emit("act",
                          (lambda pom, e8: lambda e: e.activation(
                              om[e8 % 2][:], pom[:], AF.Identity,
                              bias=bels[:, e8:e8 + 1],
                              scale=els[:, e8:e8 + 1]))(pom, e8),
                          waits=[("pe_om", None), ("els_done", 2),
                                 ("s_scp", plan.lag("s_scp", 1))],
                          inc="act_om")

            def stA(b, e8, APOS):
                omb = om[e8 % 2]
                plan.emit("dve",
                          (lambda omb: lambda e: e.tensor_mul(
                              omb[:], omb[:], pos_s[:]))(omb),
                          waits=[("act_om", None), ("apos", APOS - 16)],
                          inc="dve_oms")
                init = 0.0 if b == 0 else carr_p[:, e8:e8 + 1]
                plan.emit("dve",
                          (lambda omb, init, e8: lambda e: e.tensor_tensor_scan(
                              phi[e8 % 2][:], omb[:], omb[:], init,
                              ALU.add, ALU.bypass))(omb, init, e8),
                          waits=[("dve_cphi",
                                  8 * (b - 1) + e8 + 1 if b > 0 else 0)],
                          inc="s_scp")
                plan.emit("dve",
                          (lambda e8: lambda e: e.tensor_copy(
                              carr_p[:, e8:e8 + 1], phi[e8 % 2][:, BS - 1:BS]))(e8),
                          waits=[("s_scp", None)], inc="dve_cphi")
                ts_b, tc_b = t_s[e8 % 2], t_c[e8 % 2]
                ph = phi[e8 % 2]
                plan.emit("dve", (lambda ts_b, ph: lambda e: e.tensor_scalar_mul(
                    ts_b[:], ph[:], INV_2PI))(ts_b, ph),
                    waits=[("act_trig", plan.lag("act_trig", 2))],
                    inc=None)
                plan.emit("dve", (lambda tc_b, ph: lambda e: e.tensor_scalar(
                    tc_b[:], ph[:], INV_2PI, 0.25, ALU.mult, ALU.add))(tc_b, ph),
                    inc=None)
                plan.emit("dve", (lambda ts_b: lambda e: e.tensor_scalar(
                    rn[:], ts_b[:], MAGIC, MAGIC, ALU.add, ALU.subtract))(ts_b),
                    inc=None)
                plan.emit("dve", (lambda ts_b: lambda e: e.tensor_sub(
                    ts_b[:], ts_b[:], rn[:]))(ts_b), inc=None)
                plan.emit("dve", (lambda tc_b: lambda e: e.tensor_scalar(
                    rn[:], tc_b[:], MAGIC, MAGIC, ALU.add, ALU.subtract))(tc_b),
                    inc=None)
                plan.emit("dve", (lambda tc_b: lambda e: e.tensor_sub(
                    tc_b[:], tc_b[:], rn[:]))(tc_b), inc="dve_trig")

            def stB(b, e8):
                ts_b, tc_b = t_s[e8 % 2], t_c[e8 % 2]
                plan.emit("act", (lambda ts_b, o: lambda e: e.activation(
                    o[:], ts_b[:], AF.Sin, bias=0.0, scale=TWO_PI))(
                    ts_b, sinb[e8 % 2]),
                    waits=[("dve_trig", None),
                           ("dve_cm", plan.lag("dve_cm", 1)),
                           ("gp_uv", plan.lag("gp_uv", 1))],
                    inc="act_trig")
                plan.emit("act", (lambda tc_b, o: lambda e: e.activation(
                    o[:], tc_b[:], AF.Sin, bias=0.0, scale=TWO_PI))(
                    tc_b, cosb[e8 % 2]),
                    inc="act_trig")

            def prevq(b, e8, back, q):
                e = e8 - back
                bb = b
                if e < 0:
                    e += ND
                    bb -= 1
                return needq.get((bb, e), {}).get(q, 0)

            def stage_quarter(b, e8, q, fnq, eng="dve", extra=()):
                stg = q_st[q][e8 % 2]
                bs = b * BS
                plan.emit(eng, fnq(stg),
                    waits=[("pe_st1", plan.lag("pe_st1", 0)),
                           ("sp_ctxst",
                            plan.lag("sp_ctxst", 0 if q == 0 else 16))]
                          + list(extra),
                    inc="dve_q")
                needq.setdefault((b, e8), {})[q] = plan.cur("dve_q")
                if q == 3:
                    qa = qsta[e8 % 2]
                    plan.emit("act", (lambda e8, qa, bs: lambda e: e.dma_start(
                        out=ctx_dq[e8][:, :, bs:bs + BS].rearrange(
                            "q p c -> p q c"),
                        in_=qa[:].rearrange("p (q c) -> p q c", q=4)))(
                        e8, qa, bs),
                        waits=[("dve_q", plan.cur("dve_q"))],
                        inc="sp_ctxst", n=16)

            def stC(b, e8, AXL):
                sin_b, cos_b = sinb[e8 % 2], cosb[e8 % 2]
                xsl = xTf[:, e8 * BS:(e8 + 1) * BS]
                plan.emit("dve", (lambda cos_b, xsl: lambda e: e.tensor_mul(
                    cm_r[:], xsl, cos_b[:]))(cos_b, xsl),
                    waits=[("act_trig", None), ("axl", AXL),
                           ("dve_q", prevq(b, e8, 1, 1))],
                    inc=None)
                plan.emit("dve", (lambda sin_b, xsl: lambda e: e.tensor_mul(
                    cm_i[:], xsl, sin_b[:]))(sin_b, xsl), inc="dve_cm")
                DCM = plan.cur("dve_cm")
                stage_quarter(b, e8, 0,
                              lambda stg: (lambda e: e.activation(
                                  stg[:], cm_r[:], AF.Identity,
                                  bias=0.0, scale=1.0)),
                              eng="act", extra=[("dve_cm", DCM)])
                stage_quarter(b, e8, 1,
                              lambda stg: (lambda e: e.activation(
                                  stg[:], cm_i[:], AF.Identity,
                                  bias=0.0, scale=1.0)),
                              eng="act", extra=[("dve_cm", DCM)])
                plan.emit("dve", (lambda Sb, init: lambda e: e.tensor_tensor_scan(
                    Sb[:], cm_r[:], cm_r[:], init, ALU.add, ALU.bypass))(
                    S_r[e8 % 2], 0.0 if b == 0 else carr_r[:, e8:e8 + 1]),
                    waits=[("dve_cr", 8 * (b - 1) + e8 + 1 if b > 0 else 0),
                           ("gp_uv", plan.lag("gp_uv", 2))],
                    inc="s_scr")
                plan.emit("dve", (lambda Sb, e8: lambda e: e.tensor_copy(
                    carr_r[:, e8:e8 + 1], Sb[:, BS - 1:BS]))(S_r[e8 % 2], e8),
                    waits=[("s_scr", None)], inc="dve_cr")
                plan.emit("dve", (lambda Sb, init: lambda e: e.tensor_tensor_scan(
                    Sb[:], cm_i[:], cm_i[:], init, ALU.add, ALU.bypass))(
                    S_i[e8 % 2], 0.0 if b == 0 else carr_i[:, e8:e8 + 1]),
                    waits=[("dve_ci", 8 * (b - 1) + e8 + 1 if b > 0 else 0),
                           ("gp_uv", plan.lag("gp_uv", 2))],
                    inc="s_sci")
                plan.emit("dve", (lambda Sb, e8: lambda e: e.tensor_copy(
                    carr_i[:, e8:e8 + 1], Sb[:, BS - 1:BS]))(S_i[e8 % 2], e8),
                    waits=[("s_sci", None)], inc="dve_ci")

            def stD(b, e8):
                Srb, Sib = S_r[e8 % 2], S_i[e8 % 2]
                sin_b, cos_b = sinb[e8 % 2], cosb[e8 % 2]
                plan.emit("dve", (lambda Srb, cos_b, e8: lambda e: e.tensor_mul(
                    uv0[e8 % 2][:], Srb[:], cos_b[:]))(Srb, cos_b, e8),
                    waits=[("dve_cr", None), ("dve_ci", None),
                           ("dve_q", prevq(b, e8, 2, 3))],
                    inc=None)
                plan.emit("dve", (lambda Sib, sin_b, e8: lambda e: e.tensor_mul(
                    uv1[e8 % 2][:], Sib[:], sin_b[:]))(Sib, sin_b, e8),
                    inc="gp_uv")

            def stE(b, e8, APOS):
                u0, u1b = uv0[e8 % 2], uv1[e8 % 2]
                plan.emit("dve", (lambda u0, u1b: lambda e: e.tensor_add(
                    u0[:], u0[:], u1b[:]))(u0, u1b),
                    waits=[("gp_uv", None)], inc=None)
                stage_quarter(b, e8, 2,
                              (lambda u0: (lambda stg: (lambda e: e.tensor_mul(
                                  stg[:], u0[:], pos_i[:]))))(u0),
                              extra=[("apos", APOS)])

            def stF(b, e8):
                Srb, Sib = S_r[e8 % 2], S_i[e8 % 2]
                sin_b, cos_b = sinb[e8 % 2], cosb[e8 % 2]
                plan.emit("dve", (lambda Sib, cos_b, e8: lambda e: e.tensor_mul(
                    uv0[e8 % 2][:], Sib[:], cos_b[:]))(Sib, cos_b, e8),
                    waits=[("dve_q", needq[(b, e8)][2])], inc=None)
                plan.emit("dve", (lambda Srb, sin_b, e8: lambda e: e.tensor_mul(
                    uv1[e8 % 2][:], Srb[:], sin_b[:]))(Srb, sin_b, e8),
                    inc="gp_uv")

            def stG(b, e8):
                u0, u1b = uv0[e8 % 2], uv1[e8 % 2]
                plan.emit("dve", (lambda u0, u1b: lambda e: e.tensor_sub(
                    u0[:], u0[:], u1b[:]))(u0, u1b),
                    waits=[("gp_uv", None)], inc=None)
                stage_quarter(b, e8, 3,
                              (lambda u0: (lambda stg: (lambda e: e.tensor_mul(
                                  stg[:], u0[:], pos_i[:]))))(u0))

            def front_stats(b, e8):
                for q in range(4):
                    stg = q_st[q][e8 % 2]
                    sq = sqb[q % 2]
                    NQ = needq[(b, e8)][q]
                    plan.emit("act", (lambda stg, sq: lambda e: e.activation(
                        sq[:], stg[:], AF.Square))(stg, sq),
                        waits=[("dve_q", NQ),
                               ("pe_st1", plan.lag("pe_st1", 1))],
                        inc="act_sq")
                    first = (e8 == 0 and q == 0)
                    last = (e8 == ND - 1 and q == 3)
                    plan.emit("pe", (lambda stg, first, last: lambda e: e.matmul(
                        p_s1[:], onesb[:], stg[:], start=first, stop=last))(
                        stg, first, last),
                        waits=[("dve_q", NQ), ("els_done", ELS_ALL),
                               ("dve_var1", plan.lag("dve_var1", 0))],
                        inc=None)
                    plan.emit("pe", (lambda sq, first, last: lambda e: e.matmul(
                        p_s2[:], onesb[:], sq[:], start=first, stop=last))(
                        sq, first, last),
                        waits=[("act_sq", None)],
                        inc="pe_st1")

            def emit_ln1(b):
                muB, rstdB = muBp[b % 2], rstdBp[b % 2]
                PE_ST1 = plan.cur("pe_st1")
                plan.emit("dve",
                          (lambda muB: lambda e: e.tensor_scalar_mul(
                              muB[:], p_s1[:], 1.0 / 4096.0))(muB),
                          waits=[("pe_st1", PE_ST1),
                                 ("act_sqrt1", plan.lag("act_sqrt1", 0)),
                                 ("gp_z", plan.lag("gp_z", 16))],
                          inc=None)
                plan.emit("dve", (lambda muB: lambda e: e.tensor_mul(
                    veB[:], muB[:], muB[:]))(muB), inc=None)
                plan.emit("dve", lambda e: e.tensor_scalar(
                    veB[:], veB[:], EPS, -1.0, ALU.subtract, ALU.mult), inc=None)
                plan.emit("dve", lambda e: e.scalar_tensor_tensor(
                    veB[:], p_s2[:], 1.0 / 4096.0, veB[:], ALU.mult, ALU.add),
                    inc="dve_var1")
                plan.emit("act", (lambda rstdB: lambda e: e.activation(
                    rstdB[:], veB[:], AF.Sqrt))(rstdB),
                    waits=[("dve_var1", None), ("gp_z", plan.lag("gp_z", 16))],
                    inc="act_sqrt1")
                plan.emit("dve", (lambda rstdB: lambda e: e.reciprocal(
                    rstdB[:], rstdB[:]))(rstdB),
                    waits=[("act_sqrt1", None)], inc="dve_rstd1")

            def make_tail(b):
                bs = b * BS
                muB, rstdB = muBp[b % 2], rstdBp[b % 2]
                SPST_B = plan.cur("sp_ctxst")
                RSTD1 = plan.cur("dve_rstd1")
                PEW1_0 = plan.cur("pe_w1")
                PEW2_0 = plan.cur("pe_w2")
                cell = {}
                groups = []

                def reload_group():
                    for f4 in range(NFT // 4):
                        plan.emit("sp", (lambda f4, bs: lambda e: e.dma_start(
                            out=ctxw[:, 4 * f4 * BS:(4 * f4 + 4) * BS].rearrange(
                                "p (f c) -> p f c", f=4),
                            in_=ctx_d[4 * f4:4 * f4 + 4, :, bs:bs + BS].rearrange(
                                "f p c -> p f c")))(f4, bs),
                            waits=[("sp_ctxst", SPST_B),
                                   ("pe_w1", plan.lag("pe_w1", 0))],
                            inc="sp_ctxw", n=16)
                    cell["ctxw"] = plan.cur("sp_ctxw")
                groups.append(reload_group)

                def w1_mm(o):
                    spw = []
                    for hh in range(2):
                        wsb = w1g_sb[hh]
                        plan.emit("sp", (lambda o, hh, wsb: lambda e: e.dma_start(
                            out=wsb[:],
                            in_=w1g_d[o, :, hh * 16 * P:(hh + 1) * 16 * P]))(
                            o, hh, wsb),
                            waits=[("fold_st", N_FOLD_ST),
                                   ("pe_w1", plan.lag("pe_w1", 1 - hh))],
                            inc="sp_w1g", n=16)
                        spw.append(plan.cur("sp_w1g"))
                    pz = p_z[o % 2]
                    for ft in range(NFT):
                        hh = ft // 16
                        plan.emit("pe", (lambda pz, hh, ft: lambda e: e.matmul(
                            pz[:], w1g_sb[hh][:, (ft % 16) * P:(ft % 16 + 1) * P],
                            ctxw[:, ft * BS:(ft + 1) * BS],
                            start=(ft == 0), stop=(ft == NFT - 1)))(pz, hh, ft),
                            waits=[("sp_ctxw", cell["ctxw"]), ("sp_w1g", spw[hh]),
                                   ("act_rb", plan.lag("act_rb", 0)),
                                   ("dve_zc", plan.lag("dve_zc", 0))],
                            inc="pe_w1" if ft in (15, NFT - 1) else None)

                def w1_post(o):
                    pz = p_z[o % 2]
                    plan.emit("dve",
                              (lambda pz, o: lambda e: e.scalar_tensor_tensor(
                                  z1[:], muB[:], nw1s[:, o:o + 1], pz[:],
                                  ALU.mult, ALU.add))(pz, o),
                              waits=[("pe_w1", PEW1_0 + 2 * (o + 1)),
                                     ("fold_fin", FOLD_DONE),
                                     ("gp_z", plan.lag("gp_z", 0))],
                              inc="dve_zc")
                    plan.emit("dve", (lambda rstdB: lambda e: e.tensor_mul(
                        z2[:], z1[:], rstdB[:]))(rstdB),
                        waits=[("dve_zc", None), ("dve_rstd1", RSTD1),
                               ("act_h1a", plan.lag("act_h1a", 0))],
                        inc="gp_z")
                    plan.emit("act", (lambda o: lambda e: e.activation(
                        h1a[:, o * BS:(o + 1) * BS], z2[:], AF.Gelu,
                        bias=bias1[:, o:o + 1], scale=1.0))(o),
                        waits=[("gp_z", None),
                               ("pe_w2", plan.lag("pe_w2", 0))],
                        inc="act_h1a")

                for o in range(NO1):
                    if o == 0:
                        groups.append((lambda o: lambda: w1_mm(o))(o))
                    else:
                        groups.append((lambda o: lambda: (w1_mm(o),
                                                          w1_post(o - 1)))(o))
                groups.append(lambda: w1_post(NO1 - 1))

                def w2_mm(dd):
                    wsl2 = w2sl[dd % 2]
                    plan.emit("sp", (lambda dd, wsl2: lambda e: e.dma_start(
                        out=wsl2[:], in_=w2_d[dd]))(dd, wsl2),
                        waits=[("pe_w2", plan.lag("pe_w2", 1)),
                               ("pc_st", min(16 * (dd + 1), plan.cur("pc_st")))],
                        inc="ld_w2", n=16)
                    cell[("ldw2", dd)] = plan.cur("ld_w2")
                    xr = xres[dd % 2]
                    plan.emit("sp", (lambda dd, bs, xr: lambda e: e.dma_start(
                        out=xr[:], in_=xT[dd * P:(dd + 1) * P, bs:bs + BS]))(
                        dd, bs, xr),
                        waits=[("dve_r", plan.lag("dve_r", 0))],
                        inc="axr", n=16)
                    cell[("axr", dd)] = plan.cur("axr")
                    pr = p_z[dd % 2]
                    for ok in range(NO1):
                        plan.emit("pe", (lambda pr, wsl2, ok: lambda e: e.matmul(
                            pr[:], wsl2[:, ok * P:(ok + 1) * P],
                            h1a[:, ok * BS:(ok + 1) * BS],
                            start=(ok == 0), stop=(ok == NO1 - 1)))(pr, wsl2, ok),
                            waits=[("act_h1a", plan.cur("act_h1a")),
                                   ("ld_w2", cell[("ldw2", dd)]),
                                   ("dve_zc", plan.lag("dve_zc", 0)),
                                   ("act_rb", plan.lag("act_rb", 1))],
                            inc="pe_w2" if ok == NO1 - 1 else None)

                def w2_post(dd):
                    pr = p_z[dd % 2]
                    plan.emit("act", (lambda pr, dd, o: lambda e: e.activation(
                        o[:], pr[:], AF.Identity, bias=b2c[:, dd:dd + 1],
                        scale=1.0))(pr, dd, rbuf[dd % 2]),
                        waits=[("pe_w2", PEW2_0 + dd + 1),
                               ("dve_r", plan.lag("dve_r", 1))],
                        inc="act_rb")
                    xr = xres[dd % 2]
                    plan.emit("dve",
                              (lambda xr, dd, o: lambda e: e.scalar_tensor_tensor(
                                  r_sb[:, dd * BS:(dd + 1) * BS], xr[:], 2.0, o[:],
                                  ALU.mult, ALU.add))(xr, dd, rbuf[dd % 2]),
                              waits=[("act_rb", None), ("axr", cell[("axr", dd)]),
                                     ("dve_x2", plan.lag("dve_x2", 0))],
                              inc=None)
                    plan.emit("dve", (lambda dd: lambda e: e.tensor_copy(
                        rbf[:], r_sb[:, dd * BS:(dd + 1) * BS]))(dd),
                        waits=[("pe_st2", plan.lag("pe_st2", 0))],
                        inc="dve_r")
                    plan.emit("act", lambda e: e.activation(
                        sqr[:], rbf[:], AF.Square),
                        waits=[("dve_r", None),
                               ("pe_st2", plan.lag("pe_st2", 0))],
                        inc="act_sqr")
                    plan.emit("pe", (lambda dd: lambda e: e.matmul(
                        p_t1[:], onesb[:], rbf[:], start=(dd == 0),
                        stop=(dd == ND - 1)))(dd),
                        waits=[("dve_r", None),
                               ("dve_var2", plan.lag("dve_var2", 0))],
                        inc=None)
                    plan.emit("pe", (lambda dd: lambda e: e.matmul(
                        p_t2[:], onesb[:], sqr[:], start=(dd == 0),
                        stop=(dd == ND - 1)))(dd),
                        waits=[("act_sqr", None)],
                        inc="pe_st2")

                for dd in range(ND):
                    if dd == 0:
                        groups.append((lambda dd: lambda: w2_mm(dd))(dd))
                    else:
                        groups.append((lambda dd: lambda: (w2_mm(dd),
                                                           w2_post(dd - 1)))(dd))
                groups.append(lambda: w2_post(ND - 1))

                def ln2_group():
                    PE_ST2 = plan.cur("pe_st2")
                    plan.emit("dve",
                              lambda e: e.tensor_scalar_mul(mu2B[:], p_t1[:],
                                                            1.0 / 1024.0),
                              waits=[("pe_st2", PE_ST2)], inc=None)
                    plan.emit("dve", lambda e: e.tensor_mul(ve2B[:], mu2B[:], mu2B[:]),
                              inc=None)
                    plan.emit("dve", lambda e: e.tensor_scalar(
                        ve2B[:], ve2B[:], EPS, -1.0, ALU.subtract, ALU.mult),
                        inc=None)
                    plan.emit("dve", lambda e: e.scalar_tensor_tensor(
                        ve2B[:], p_t2[:], 1.0 / 1024.0, ve2B[:], ALU.mult, ALU.add),
                        inc="dve_var2")
                    plan.emit("act", lambda e: e.activation(rstd2B[:], ve2B[:],
                                                            AF.Sqrt),
                              waits=[("dve_var2", None)], inc="act_sqrt2")
                    plan.emit("dve", lambda e: e.reciprocal(rstd2B[:], rstd2B[:]),
                              waits=[("act_sqrt2", None)], inc="dve_rstd2")
                groups.append(ln2_group)

                def apply_group(dd):
                    xf = x2fs[dd % 2]
                    xb = x2bs[dd % 2]
                    plan.emit("dve", (lambda dd, xf: lambda e: e.tensor_sub(
                        xf[:], r_sb[:, dd * BS:(dd + 1) * BS], mu2B[:]))(dd, xf),
                        waits=[("dve_rstd2", None),
                               ("sp_x2st", plan.lag("sp_x2st", 32))],
                        inc=None)
                    plan.emit("dve", (lambda xf: lambda e: e.tensor_mul(
                        xf[:], xf[:], rstd2B[:]))(xf), inc=None)
                    plan.emit("act", (lambda xf, xb: lambda e: e.activation(
                        xb[:], xf[:], AF.Identity, bias=0.0, scale=1.0))(xf, xb),
                        waits=[("ch_dve", plan.cur("ch_dve"))], inc="dve_x2")
                    plan.emit("act", (lambda dd, xf, bs: lambda e: e.dma_start(
                        out=x2f_d[dd, :, bs:bs + BS], in_=xf[:]))(dd, xf, bs),
                        waits=[("dve_x2", None)], inc="sp_x2st", n=16)
                    plan.emit("act", (lambda dd, xb, bs: lambda e: e.dma_start(
                        out=x2b_d[dd, :, bs:bs + BS], in_=xb[:]))(dd, xb, bs),
                        inc="sp_x2st", n=16)
                for dd in range(ND):
                    groups.append((lambda dd: lambda: apply_group(dd))(dd))
                return groups

            pending = []
            for b in range(0 if not SKIP_ATT else NB, NB):
                bs = b * BS
                plan.section = "front"
                plan.emit("sp",
                          (lambda bs: lambda e: e.dma_start(
                              out=xTf[:].rearrange("p (k s) -> p k s", k=ND),
                              in_=xT[:, bs:bs + BS].rearrange(
                                  "(k p) s -> p k s", p=P)))(bs),
                          waits=[("dve_cm", plan.lag("dve_cm", 0)),
                                 ("axlb", plan.lag("axlb", 0)),
                                 ("fold_fin", FOLD_DONE)],
                          inc="axl", n=16)
                plan.emit("sp", (lambda bs: lambda e: e.dma_start(
                    out=pos_s[:], in_=posscale[:, bs:bs + BS]))(bs),
                    waits=[("dve_oms", plan.lag("dve_oms", 0))], inc="apos", n=16)
                plan.emit("sp", (lambda bs: lambda e: e.dma_start(
                    out=pos_i[:], in_=posinv[:, bs:bs + BS]))(bs),
                    waits=[("dve_q", plan.lag("dve_q", 0))], inc="apos", n=16)
                AXL = plan.cur("axl")
                plan.emit("dve", lambda e: e.tensor_copy(xTb[:], xTf[:]),
                          waits=[("axl", AXL),
                                 ("pe_om", plan.lag("pe_om", 0))],
                          inc="axlb")
                AXLB = plan.cur("axlb")
                APOS = plan.cur("apos")

                emit_precast(PC_PER_BLOCK)

                gi = 0
                NIT = ND + 2
                for i in range(NIT):
                    take = (len(pending) * (i + 1) + NIT - 1) // NIT
                    while gi < min(take, len(pending)):
                        pending[gi]()
                        gi += 1
                    if i < ND:
                        front_pe(b, i, AXLB)
                    if i >= 1 and i - 1 < ND:
                        stB(b, i - 1)
                    if i < ND:
                        front_evac(b, i)
                    if i >= 3 and i - 3 < ND:
                        front_stats(b, i - 3)
                    if i >= 2 and i - 2 < ND:
                        stE(b, i - 2, APOS)
                        stF(b, i - 2)
                    if i < ND:
                        stA(b, i, APOS)
                    if i >= 1 and i - 1 < ND:
                        stC(b, i - 1, AXL)
                    if i >= 2 and i - 2 < ND:
                        stG(b, i - 2)
                    if i >= 1 and i - 1 < ND:
                        stD(b, i - 1)
                while gi < len(pending):
                    pending[gi]()
                    gi += 1
                front_stats(b, ND - 1)
                plan.section = "ln1"
                emit_ln1(b)
                pending = make_tail(b)
            for g in pending:
                g()

            emit_precast(n_pcjobs)
            N_PC = plan.cur("pc_st")
            N_X2 = plan.cur("sp_x2st")
            plan.barrier()

        sB = ExitStack()
        with sB:
            def sb2(name, shape, dtype):
                return sB.enter_context(nc.sbuf_tensor(name, shape, dtype))

            def ps2(name, shape):
                return sB.enter_context(nc.psum_tensor(name, shape, F32))

            x2b_sb = [sb2(f"x2b_sb{i}", [P, ND * FBS], BF16) for i in range(2)]
            wsl = [sb2(f"wsl{i}", [P, ND * P], BF16) for i in range(4)]
            hsl = [sb2(f"hsl{i}", [P, NH * P], BF16) for i in range(2)]
            t1b = [sb2(f"t1b{i}", [P, FBS], F32) for i in range(2)]
            g_sb = [sb2(f"g_sb{i}", [P, NH * FBS], BF16) for i in range(2)]
            hb_sb = sb2("hb_sb", [P, NH * FBS], BF16)
            x2f_sl = [sb2(f"x2f_sl{i}", [P, FBS], F32) for i in range(2)]
            r3b = [sb2(f"r3b{i}", [P, FBS], BF16) for i in range(2)]
            sq3 = [sb2(f"sq3{i}", [P, FBS], BF16) for i in range(2)]
            r3s = [sb2(f"r3s{i}", [P, FBS], F32) for i in range(2)]
            mu3B = sb2("mu3B", [P, FBS], F32)
            ve3B = sb2("ve3B", [P, FBS], F32)
            rstd3B = sb2("rstd3B", [P, FBS], F32)
            r3l = [sb2(f"r3l{i}", [P, FBS], F32) for i in range(2)]
            out_s = [sb2(f"out_s{i}", [P, FBS], F32) for i in range(2)]

            f_z1 = [ps2(f"f_z1{i}", [P, FBS]) for i in range(2)]
            f_z2 = [ps2(f"f_z2{i}", [P, FBS]) for i in range(2)]
            f_ph = [ps2(f"f_ph{i}", [P, FBS]) for i in range(2)]
            f_p4 = [ps2(f"f_p4{i}", [P, FBS]) for i in range(2)]

            for fb in range(0 if not SKIP_FFN else NFB, NFB):
                bs = fb * FBS
                plan.section = "ffn1"
                x2b = x2b_sb[fb % 2]
                gb = g_sb[fb % 2]
                plan.emit("sp", (lambda bs, x2b: lambda e: e.dma_start(
                    out=x2b[:].rearrange("p (k c) -> p k c", k=ND),
                    in_=x2b_d[:, :, bs:bs + FBS].rearrange("k p c -> p k c")))(
                    bs, x2b),
                    waits=[("sp_x2st", N_X2),
                           ("f_pe1", plan.lag("f_pe1", NH))],
                    inc="f_x2l", n=16)
                F_X2L = plan.cur("f_x2l")
                for og in range(NH):
                    w1s, w2s = wsl[2 * (og % 2)], wsl[2 * (og % 2) + 1]
                    plan.emit("sp", (lambda og, w1s: lambda e: e.dma_start(
                        out=w1s[:], in_=fw1_d[og]))(og, w1s),
                        waits=[("pc_st", min(N_PC, plan.cur("pc_st"))),
                               ("f_pe1", plan.lag("f_pe1", 0 if og == 1 else 1))],
                        inc="f_wl", n=16)
                    plan.emit("sp", (lambda og, w2s: lambda e: e.dma_start(
                        out=w2s[:], in_=fw2_d[og]))(og, w2s),
                        inc="f_wl", n=16)
                    pz1, pz2 = f_z1[og % 2], f_z2[og % 2]
                    for dk in range(ND):
                        plan.emit("pe", (lambda pz1, w1s, dk, x2b: lambda e: e.matmul(
                            pz1[:], w1s[:, dk * P:(dk + 1) * P],
                            x2b[:, dk * FBS:(dk + 1) * FBS],
                            start=(dk == 0), stop=(dk == ND - 1)))(pz1, w1s, dk, x2b),
                            waits=[("f_x2l", F_X2L), ("f_wl", None),
                                   ("f_act1", plan.lag("f_act1", 1))],
                            inc=None)
                    for dk in range(ND):
                        plan.emit("pe", (lambda pz2, w2s, dk, x2b: lambda e: e.matmul(
                            pz2[:], w2s[:, dk * P:(dk + 1) * P],
                            x2b[:, dk * FBS:(dk + 1) * FBS],
                            start=(dk == 0), stop=(dk == ND - 1)))(pz2, w2s, dk, x2b),
                            waits=[("f_dve1", plan.lag("f_dve1", 1))],
                            inc="f_pe1" if dk == ND - 1 else None)
                    t1 = t1b[og % 2]
                    plan.emit("act", (lambda pz1, og, t1: lambda e: e.activation(
                        t1[:], pz1[:], AF.Silu, bias=0.0,
                        scale=s1sb[:, og:og + 1]))(pz1, og, t1),
                        waits=[("f_pe1", None), ("ld_small", N_SMALL)],
                        inc="f_act1")
                    plan.emit("dve",
                              (lambda pz2, og, t1, gb: lambda e: e.scalar_tensor_tensor(
                                  gb[:, og * FBS:(og + 1) * FBS], pz2[:],
                                  s2sb[:, og:og + 1], t1[:],
                                  ALU.mult, ALU.mult))(pz2, og, t1, gb),
                              waits=[("f_act1", None),
                                     ("f_pe4", plan.lag("f_pe4", ND))],
                              inc="f_dve1")
                F_DVE1 = plan.cur("f_dve1")
                plan.section = "ffn2"
                for o2 in range(NH):
                    hs = hsl[o2 % 2]
                    plan.emit("sp", (lambda o2, hs: lambda e: e.dma_start(
                        out=hs[:], in_=fh1_d[o2]))(o2, hs),
                        waits=[("pc_st", min(N_PC, plan.cur("pc_st"))),
                               ("f_pe2", plan.lag("f_pe2", 1)),
                               ("f_pe4", plan.lag("f_pe4", 0 if o2 == 1 else 1))],
                        inc="f_wl", n=16)
                    ph = f_ph[o2 % 2]
                    if o2 < 2:
                        pass
                    for kt in range(NH):
                        plan.emit("pe", (lambda ph, hs, kt, gb: lambda e: e.matmul(
                            ph[:], hs[:, kt * P:(kt + 1) * P],
                            gb[:, kt * FBS:(kt + 1) * FBS],
                            start=(kt == 0), stop=(kt == NH - 1)))(ph, hs, kt, gb),
                            waits=[("f_dve1", F_DVE1), ("f_wl", None),
                                   ("f_var3", plan.lag("f_var3", 0)),
                                   ("f_act2", plan.lag("f_act2", 1))],
                            inc="f_pe2" if kt == NH - 1 else None)
                    plan.emit("act", (lambda ph, o2: lambda e: e.activation(
                        hb_sb[:, o2 * FBS:(o2 + 1) * FBS], ph[:], AF.Gelu,
                        bias=0.0, scale=hs1sb[:, o2:o2 + 1]))(ph, o2),
                        waits=[("f_pe2", None)], inc="f_act2")
                F_ACT2 = plan.cur("f_act2")
                plan.section = "ffn3"
                for o3 in range(NH):
                    hs = hsl[o3 % 2]
                    plan.emit("sp", (lambda o3, hs: lambda e: e.dma_start(
                        out=hs[:], in_=fh2_d[o3]))(o3, hs),
                        waits=[("f_pe3", plan.lag("f_pe3", 1)),
                               ("f_pe2", plan.lag("f_pe2", 0 if o3 == 1 else 1))],
                        inc="f_wl", n=16)
                    ph = f_ph[o3 % 2]
                    for kt in range(NH):
                        plan.emit("pe", (lambda ph, hs, kt: lambda e: e.matmul(
                            ph[:], hs[:, kt * P:(kt + 1) * P],
                            hb_sb[:, kt * FBS:(kt + 1) * FBS],
                            start=(kt == 0), stop=(kt == NH - 1)))(ph, hs, kt),
                            waits=[("f_act2", F_ACT2), ("f_wl", None),
                                   ("f_act3", plan.lag("f_act3", 1))],
                            inc="f_pe3" if kt == NH - 1 else None)
                    plan.emit("act", (lambda ph, o3, gb: lambda e: e.activation(
                        gb[:, o3 * FBS:(o3 + 1) * FBS], ph[:], AF.Gelu,
                        bias=0.0, scale=hs2sb[:, o3:o3 + 1]))(ph, o3, gb),
                        waits=[("f_pe3", None)], inc="f_act3")
                F_ACT3 = plan.cur("f_act3")
                plan.section = "ffn4"
                for dd in range(ND):
                    hs = hsl[dd % 2]
                    plan.emit("sp", (lambda dd, hs: lambda e: e.dma_start(
                        out=hs[:], in_=fw3_d[dd]))(dd, hs),
                        waits=[("f_pe4", plan.lag("f_pe4", 1)),
                               ("f_pe3", plan.lag("f_pe3", 0 if dd == 1 else 1))],
                        inc="f_wl", n=16)
                    plan.emit("gp", (lambda dd, bs, x: lambda e: e.dma_start(
                        out=x[:], in_=x2f_d[dd, :, bs:bs + FBS]))(
                        dd, bs, x2f_sl[dd % 2]),
                        waits=[("f_dve4", plan.lag("f_dve4", 1))],
                        inc="f_x2fl", n=16)
                    p4 = f_p4[dd % 2]
                    for kt in range(NH):
                        plan.emit("pe", (lambda p4, hs, kt, gb: lambda e: e.matmul(
                            p4[:], hs[:, kt * P:(kt + 1) * P],
                            gb[:, kt * FBS:(kt + 1) * FBS],
                            start=(kt == 0), stop=(kt == NH - 1)))(p4, hs, kt, gb),
                            waits=[("f_act3", F_ACT3), ("f_wl", None),
                                   ("f_dve4", plan.lag("f_dve4", 1))],
                            inc="f_pe4" if kt == NH - 1 else None)
                    r3 = r3s[dd % 2]
                    plan.emit("dve",
                              (lambda p4, dd, r3, x: lambda e: e.scalar_tensor_tensor(
                                  r3[:], p4[:], s3c[:, dd:dd + 1], x[:],
                                  ALU.mult, ALU.add))(p4, dd, r3, x2f_sl[dd % 2]),
                              waits=[("f_pe4", None), ("f_x2fl", None),
                                     ("f_r3st", plan.lag("f_r3st", 32)),
                                     ("ld_small", N_SMALL)],
                              inc=None)
                    plan.emit("dve", (lambda r3, dd: lambda e: e.tensor_copy(
                        r3b[dd % 2][:], r3[:]))(r3, dd),
                        waits=[("f_st3", plan.lag("f_st3", 1))],
                        inc="f_dve4")
                    plan.emit("act", (lambda r3, dd, bs: lambda e: e.dma_start(
                        out=r3_d[dd, :, bs:bs + FBS], in_=r3[:]))(r3, dd, bs),
                        waits=[("f_dve4", None)], inc="f_r3st", n=16)
                    plan.emit("act", (lambda dd: lambda e: e.activation(
                        sq3[dd % 2][:], r3b[dd % 2][:], AF.Square))(dd),
                        waits=[("f_dve4", None),
                               ("f_st3", plan.lag("f_st3", 1))],
                        inc="f_sq3")
                    plan.emit("pe", (lambda dd: lambda e: e.matmul(
                        f_ph[0][:], onesb[:], r3b[dd % 2][:],
                        start=(dd == 0), stop=(dd == ND - 1)))(dd),
                        waits=[("f_dve4", None),
                               ("f_var3", plan.lag("f_var3", 0))],
                        inc=None)
                    plan.emit("pe", (lambda dd: lambda e: e.matmul(
                        f_ph[1][:], onesb[:], sq3[dd % 2][:],
                        start=(dd == 0), stop=(dd == ND - 1)))(dd),
                        waits=[("f_sq3", None)],
                        inc="f_st3")
                plan.section = "ffnln"
                F_ST3 = plan.cur("f_st3")
                plan.emit("dve",
                          lambda e: e.tensor_scalar_mul(mu3B[:], f_ph[0][:],
                                                        1.0 / 1024.0),
                          waits=[("f_st3", F_ST3)], inc=None)
                plan.emit("dve", lambda e: e.tensor_mul(ve3B[:], mu3B[:], mu3B[:]),
                          inc=None)
                plan.emit("dve", lambda e: e.tensor_scalar(
                    ve3B[:], ve3B[:], EPS, -1.0, ALU.subtract, ALU.mult), inc=None)
                plan.emit("dve", lambda e: e.scalar_tensor_tensor(
                    ve3B[:], f_ph[1][:], 1.0 / 1024.0, ve3B[:], ALU.mult, ALU.add),
                    inc="f_var3")
                plan.emit("act", lambda e: e.activation(rstd3B[:], ve3B[:], AF.Sqrt),
                          waits=[("f_var3", None)], inc="f_sqrt3")
                plan.emit("dve", lambda e: e.reciprocal(rstd3B[:], rstd3B[:]),
                          waits=[("f_sqrt3", None)], inc="f_rstd3")
                for dd in range(ND):
                    rl = r3l[dd % 2]
                    plan.emit("gp", (lambda dd, bs, rl: lambda e: e.dma_start(
                        out=rl[:], in_=r3_d[dd, :, bs:bs + FBS]))(dd, bs, rl),
                        waits=[("f_r3st", plan.cur("f_r3st") - 16 * (ND - 1 - dd)),
                               ("f_out", plan.lag("f_out", 1))],
                        inc="f_r3l", n=16)
                    ob = out_s[dd % 2]
                    plan.emit("dve", (lambda rl, ob: lambda e: e.tensor_sub(
                        ob[:], rl[:], mu3B[:]))(rl, ob),
                        waits=[("f_rstd3", None), ("f_r3l", None),
                               ("f_outst", plan.lag("f_outst", 16))],
                        inc=None)
                    plan.emit("dve", (lambda ob: lambda e: e.tensor_mul(
                        ob[:], ob[:], rstd3B[:]))(ob), inc="f_out")
                    plan.emit("act", (lambda dd, bs, ob: lambda e: e.dma_start(
                        out=outT[dd, :, bs:bs + FBS], in_=ob[:]))(dd, bs, ob),
                        waits=[("f_out", None)], inc="f_outst", n=16)

        N_OUT = plan.cur("f_outst")
        plan.emit("sp", lambda e: None, waits=[("f_outst", N_OUT)])

        with nc.Block() as block:
            @block.sync
            def _(eng):
                plan.replay("sp", eng, sems)

            @block.gpsimd
            def _(eng):
                plan.replay("gp", eng, sems)

            @block.vector
            def _(eng):
                plan.replay("dve", eng, sems)

            @block.scalar
            def _(eng):
                plan.replay("act", eng, sems)

            @block.tensor
            def _(eng):
                plan.replay("pe", eng, sems)

    return nc


def _prep_inputs(inputs):
    f32 = np.float32
    asc = np.ascontiguousarray

    def colsplit(v, ntiles):
        return asc(np.asarray(v, f32).reshape(ntiles, P).T)

    x = np.asarray(inputs["L0_attn_x"], dtype=f32)
    shared = dict(
        womT=asc(np.asarray(inputs["L0_attn_W_omega"], f32).T),
        w1Ta=asc(np.asarray(inputs["L0_attn_W1"], f32).T),
        w2Ta=asc(np.asarray(inputs["L0_attn_W2"], f32).T),
        fw1T=asc(np.asarray(inputs["L0_ffn_w1_w"], f32).T),
        fw2T=asc(np.asarray(inputs["L0_ffn_w2_w"], f32).T),
        fh1T=asc(np.asarray(inputs["L0_ffn_h1_w"], f32).T),
        fh2T=asc(np.asarray(inputs["L0_ffn_h2_w"], f32).T),
        fw3T=asc(np.asarray(inputs["L0_ffn_w3_w"], f32).T),
        bom_c=colsplit(inputs["L0_attn_b_omega"], ND),
        ls_c=colsplit(inputs["L0_attn_log_scale"], ND),
        gam_c=colsplit(inputs["L0_attn_ln_gamma"], NFT),
        bet_c=colsplit(inputs["L0_attn_ln_beta"], NFT),
        b1_c=colsplit(inputs["L0_attn_b1"], NO1),
        b2_c=colsplit(inputs["L0_attn_b2"], ND),
        s1_c=colsplit(inputs["L0_ffn_w1_s"], NH),
        s2_c=colsplit(inputs["L0_ffn_w2_s"], NH),
        hs1_c=colsplit(inputs["L0_ffn_h1_s"], NH),
        hs2_c=colsplit(inputs["L0_ffn_h2_s"], NH),
        s3_c=colsplit(inputs["L0_ffn_w3_s"], ND),
    )
    shared["sel2_in"] = np.eye(2, dtype=f32)
    pos = np.arange(1, S + 1, dtype=np.float64)
    shared["posscale"] = asc(np.broadcast_to(
        (1.0 / np.sqrt(pos)).astype(f32), (P, S)))
    shared["posinv"] = asc(np.broadcast_to(
        (1.0 / pos).astype(f32), (P, S)))

    in_maps = []
    for i in range(8):
        m = dict(shared)
        m["xT"] = asc(x[i].T)
        in_maps.append(m)
    return in_maps


_NC = None


def kernel(**inputs):
    global _NC
    if _NC is None:
        _NC = build_nc()
    in_maps = _prep_inputs(inputs)
    res = run_bass_kernel_spmd(_NC, in_maps, core_ids=list(range(8)))
    out = np.empty((8, S, D), dtype=np.float32)
    for i in range(8):
        oT = np.asarray(res.results[i]["outT"])
        out[i] = oT.reshape(D, S).T
    return out


if __name__ == "__main__":
    nc = build_nc()
    print("instructions:", nc.n_instructions())


# version 13
# speedup vs baseline: 1.0617x; 1.0617x over previous
import numpy as np
import concourse.bass as bass
import concourse.mybir as mybir
from concourse.bass_utils import run_bass_kernel_spmd
from contextlib import ExitStack

F32 = mybir.dt.float32
BF16 = mybir.dt.bfloat16
AF = mybir.ActivationFunctionType
ALU = mybir.AluOpType

P = 128
S = 4096
D = 1024
ND = D // P
BS = 512
NB = S // BS
FBS = 512
NFB = S // FBS
DH = 4096
NH = DH // P
NO1 = 2048 // P
NFT = 4096 // P
MAGIC = float(np.float32(1.5 * 2.0 ** 23))
TWO_PI = float(2.0 * np.pi)
INV_2PI = float(1.0 / (2.0 * np.pi))
EPS = 1e-5

ENGINES = ("sp", "act", "pe", "dve", "gp")
DBG = False
PARANOID = False
SKIP_FFN = False
SKIP_ATT = False
SKIP_PC = False
PC_PER_BLOCK = 18
PARANOID_SECTIONS = set()


class Plan:

    CHAINED = {"dve": "ch_dve", "act": "ch_act", "sp": "ch_sp"}

    class _MockIns:
        def then_inc(self, *a, **k):
            return self

    class _MockEng:

        def __init__(self):
            self.writes = []
            self.reads = []

        def __getattr__(self, name):
            rec_writes, rec_reads = self.writes, self.reads

            def rec(*args, **kwargs):
                outs, ins = [], []
                if "out" in kwargs:
                    outs.append(kwargs["out"])
                pos = [a for a in args if hasattr(a, "tensor")]
                if not outs and pos:
                    outs.append(pos[0])
                    pos = pos[1:]
                ins.extend(pos)
                for k, v in kwargs.items():
                    if k != "out" and hasattr(v, "tensor"):
                        ins.append(v)
                rec_writes.extend(a.tensor.name for a in outs)
                rec_reads.extend(a.tensor.name for a in ins)
                return Plan._MockIns()

            return rec

    QDEPTHS = {"sp": 20, "gpd": 28, "ac": 28}

    ENGINE_CHAINS = {
        "dve": ("ch_dve",),
        "act": ("ch_act",) + tuple(f"q_ac{j}" for j in range(QDEPTHS["ac"])),
        "sp": tuple(f"q_sp{j}" for j in range(QDEPTHS["sp"])),
        "gp": ("ch_gpc",) + tuple(f"q_gpd{j}" for j in range(QDEPTHS["gpd"])),
    }

    def __init__(self):
        self.ops = {e: [] for e in ENGINES}
        self.cnt = {}
        self.lmap = {}
        self.dep = {}
        self.qidx = {}
        self.section = "setup"

    def cur(self, sem):
        return self.cnt.get(sem, 0)

    def lag(self, sem, k):
        if PARANOID or self.section in PARANOID_SECTIONS:
            k = 0
        return max(0, self.cur(sem) - k)

    def _dma_queue(self, eng):
        return {"sp": "sp", "gp": "gpd", "act": "ac"}.get(eng)

    def _chain_of(self, eng, n):
        if n == 16:
            q = self._dma_queue(eng)
            i = self.qidx.get(q, 0)
            return f"q_{q}{i % self.QDEPTHS[q]}"
        if eng == "gp":
            return "ch_gpc"
        return self.CHAINED.get(eng)

    def barrier(self):
        snap = []
        for sem in ("ch_dve", "ch_act", "ch_gpc"):
            if self.cur(sem) > 0:
                snap.append((sem, self.cur(sem)))
        for q in ("sp", "gpd", "ac"):
            for j in range(self.QDEPTHS[q]):
                nm = f"q_{q}{j}"
                if self.cur(nm) > 0:
                    snap.append((nm, self.cur(nm)))
        for key, v in list(self.cnt.items()):
            if isinstance(key, tuple) and key[0] == "real" and v > 0:
                snap.append((key[1], v))
        self.pending_barrier = {e: list(snap) for e in ENGINES}

    def emit(self, eng, fn, waits=(), inc=None, n=1):
        w = []
        pb = getattr(self, "pending_barrier", None)
        if pb and pb.get(eng):
            for sem, val in pb.pop(eng):
                w.append((sem, val))
        for sem, val in waits:
            if val is None:
                val = self.cur(sem)
            if val <= 0:
                continue
            if sem in self.lmap:
                need = {}
                real = 0
                for lc, ch, va in self.lmap[sem]:
                    if ch is None:
                        real = va
                    else:
                        need[ch] = va
                    if lc >= val:
                        break
                else:
                    raise AssertionError(f"wait on future {sem} >= {val}")
                for ch, va in need.items():
                    w.append((ch, va))
                if real > 0:
                    w.append((sem, real))
            else:
                w.append((sem, val))
        chain = self._chain_of(eng, n)
        if chain is not None:
            if n == 16:
                q = self._dma_queue(eng)
                self.qidx[q] = self.qidx.get(q, 0) + 1
                prev = self.cur(chain)
                if prev > 0:
                    w.append((chain, prev))
            self.cnt[chain] = self.cur(chain) + n
            mock = Plan._MockEng()
            fn(mock)
            my_chains = self.ENGINE_CHAINS[eng]
            chain_after = self.cur(chain)
            need = {}
            for t in mock.reads:
                e = self.dep.get(t)
                if e and e["w"] and e["w"][0] in my_chains:
                    c, v = e["w"]
                    need[c] = max(need.get(c, 0), v)
            for t in mock.writes:
                e = self.dep.get(t)
                if e:
                    if e["w"] and e["w"][0] in my_chains:
                        c, v = e["w"]
                        need[c] = max(need.get(c, 0), v)
                    for c, v in e["r"].items():
                        if c in my_chains:
                            need[c] = max(need.get(c, 0), v)
            for c, v in need.items():
                if c == chain:
                    v = min(v, chain_after - n)
                if v > 0:
                    w.append((c, v))
            for t in mock.reads:
                e = self.dep.setdefault(t, {"w": None, "r": {}})
                e["r"][chain] = max(e["r"].get(chain, 0), chain_after)
            for t in mock.writes:
                e = self.dep.setdefault(t, {"w": None, "r": {}})
                e["w"] = (chain, chain_after)
        real_inc = None
        if inc is not None:
            self.cnt[inc] = self.cur(inc) + n
            ent = self.lmap.setdefault(inc, [])
            if chain is not None:
                ent.append((self.cur(inc), chain, self.cur(chain)))
            else:
                rc = self.cnt.get(("real", inc), 0) + n
                self.cnt[("real", inc)] = rc
                ent.append((self.cur(inc), None, rc))
                real_inc = inc
        self.ops[eng].append((fn, w, real_inc, n, chain))

    def replay(self, eng_name, eng, sems):
        waited = {}
        for fn, waits, inc, n, chain in self.ops[eng_name]:
            for sem, val in waits:
                if waited.get(sem, 0) < val:
                    eng.wait_ge(sems[sem], val)
                    waited[sem] = val
            ins = fn(eng)
            if ins is None:
                continue
            if chain is not None:
                ins.then_inc(sems[chain], n)
            elif inc is not None:
                ins.then_inc(sems[inc], n)


def build_nc():
    nc = bass.Bass("TRN2", target_bir_lowering=False)

    def din(name, shape, dtype=F32):
        return nc.dram_tensor(name, shape, dtype, kind="ExternalInput").ap()

    xT = din("xT", [D, S])
    womT = din("womT", [D, D])
    w1Ta = din("w1Ta", [4096, 2048])
    w2Ta = din("w2Ta", [2048, 1024])
    fw1T = din("fw1T", [D, DH])
    fw2T = din("fw2T", [D, DH])
    fh1T = din("fh1T", [DH, DH])
    fh2T = din("fh2T", [DH, DH])
    fw3T = din("fw3T", [DH, D])
    bom_c = din("bom_c", [P, ND])
    ls_c = din("ls_c", [P, ND])
    gam_c = din("gam_c", [P, NFT])
    bet_c = din("bet_c", [P, NFT])
    b1_c = din("b1_c", [P, NO1])
    b2_c = din("b2_c", [P, ND])
    s1_c = din("s1_c", [P, NH])
    s2_c = din("s2_c", [P, NH])
    hs1_c = din("hs1_c", [P, NH])
    hs2_c = din("hs2_c", [P, NH])
    s3_c = din("s3_c", [P, ND])
    sel2_in = din("sel2_in", [2, 2])
    posscale = din("posscale", [P, S])
    posinv = din("posinv", [P, S])

    outT = nc.dram_tensor("outT", [ND, P, S], F32, kind="ExternalOutput").ap()

    w1g_d = nc.dram_tensor("w1g_d", [NO1, P, NFT * P], BF16, kind="Internal").ap()
    wom_d = nc.dram_tensor("wom_d", [ND, P, ND * P], BF16, kind="Internal").ap()
    w2_d = nc.dram_tensor("w2_d", [ND, P, NO1 * P], BF16, kind="Internal").ap()
    fw1_d = nc.dram_tensor("fw1_d", [NH, P, ND * P], BF16, kind="Internal").ap()
    fw2_d = nc.dram_tensor("fw2_d", [NH, P, ND * P], BF16, kind="Internal").ap()
    fh1_d = nc.dram_tensor("fh1_d", [NH, P, NH * P], BF16, kind="Internal").ap()
    fh2_d = nc.dram_tensor("fh2_d", [NH, P, NH * P], BF16, kind="Internal").ap()
    fw3_d = nc.dram_tensor("fw3_d", [ND, P, NH * P], BF16, kind="Internal").ap()
    ctx_d = nc.dram_tensor("ctx_d", [NFT, P, S], BF16, kind="Internal").ap()
    x2f_d = nc.dram_tensor("x2f_d", [ND, P, S], F32,
                           kind="ExternalOutput" if DBG else "Internal").ap()
    ctxdbg = (nc.dram_tensor("ctxdbg", [NFT, P, S], BF16, kind="ExternalOutput").ap()
              if DBG else None)
    lndbg = (nc.dram_tensor("lndbg", [NB, 3, P, BS], F32, kind="ExternalOutput").ap()
             if DBG else None)
    folddbg = (nc.dram_tensor("folddbg", [P, 2 * NO1], F32,
                              kind="ExternalOutput").ap() if DBG else None)
    rowdbg = (nc.dram_tensor("rowdbg", [2, 2048], F32,
                             kind="ExternalOutput").ap() if DBG else None)
    combodbg = (nc.dram_tensor("combodbg", [P, 2 * NFT], BF16,
                               kind="ExternalOutput").ap() if DBG else None)
    bgdbg = (nc.dram_tensor("bgdbg", [P, NFT], F32,
                            kind="ExternalOutput").ap() if DBG else None)
    x2b_d = nc.dram_tensor("x2b_d", [ND, P, S], BF16, kind="Internal").ap()
    r3_d = nc.dram_tensor("r3_d", [ND, P, S], F32,
                          kind="ExternalOutput" if DBG else "Internal").ap()

    plan = Plan()
    sem_names = [
        "ch_dve", "ch_act", "ch_gpc",
        "ld_small", "ld_wom", "ld_womsl", "ld_w2", "els_done",
        "fold_ld", "fold_dve", "fold_st", "fold_pe", "fold_fin",
        "pc_ld", "pc_st",
        "axl", "axlb", "apos",
        "pe_om", "act_om", "dve_oms",
        "s_scp", "s_scr", "s_sci", "dve_cphi", "dve_cr", "dve_ci",
        "dve_trig", "act_trig", "dve_cm", "dve_q", "gp_uv", "act_sq",
        "pe_st1", "dve_var1", "act_sqrt1", "dve_rstd1",
        "sp_w1g", "pe_w1", "dve_zc", "gp_z", "act_h1a",
        "sp_ctxw", "sp_ctxst",
        "pe_w2", "act_rb", "dve_r", "act_sqr", "pe_st2", "axr", "gp_x2b",
        "dve_var2", "act_sqrt2", "dve_rstd2", "dve_x2", "sp_x2st",
        "f_x2l", "f_wl", "f_pe1", "f_dve1", "f_act1",
        "f_pe2", "f_act2", "f_pe3", "f_act3",
        "f_pe4", "f_x2fl", "f_dve4", "f_r3st", "f_sq3", "f_st3",
        "f_var3", "f_sqrt3", "f_rstd3", "f_r3l", "f_out", "f_outst", "dbg",
    ]

    sem_names += ([f"q_sp{j}" for j in range(20)] + [f"q_gpd{j}" for j in range(28)]
                  + [f"q_ac{j}" for j in range(28)])
    outer = ExitStack()
    with outer:
        class LazySems(dict):
            def __missing__(self, nm):
                h = outer.enter_context(nc.semaphore(nm))
                self[nm] = h
                return h
        sems = LazySems()

        def sbo(name, shape, dtype):
            return outer.enter_context(nc.sbuf_tensor(name, shape, dtype))

        cols = sbo("cols", [P, 8 * 16], F32)
        bom = cols[:, 0:8]
        lsc = cols[:, 16:24]
        els = cols[:, 32:40]
        bels = cols[:, 48:56]
        b2c = cols[:, 64:72]
        s3c = cols[:, 80:88]
        b1sb = sbo("b1sb", [P, NO1], F32)
        s1sb = sbo("s1sb", [P, NH], F32)
        s2sb = sbo("s2sb", [P, NH], F32)
        hs1sb = sbo("hs1sb", [P, NH], F32)
        hs2sb = sbo("hs2sb", [P, NH], F32)
        gamsb = sbo("gamsb", [P, NFT], F32)
        betsb = sbo("betsb", [P, NFT], F32)
        bgsb = sbo("bgsb", [P, NFT], F32)
        nw1s = sbo("nw1s", [P, NO1], F32)
        bias1 = sbo("bias1", [P, NO1], F32)
        onesb = sbo("onesb", [P, P], BF16)
        ones1 = sbo("ones1", [P, 4], F32)
        carr_p = sbo("carr_p", [P, ND], F32)
        carr_r = sbo("carr_r", [P, ND], F32)
        carr_i = sbo("carr_i", [P, ND], F32)
        sel2 = sbo("sel2", [2, 2], F32)

        def dma(dst, src):
            return lambda eng: eng.dma_start(out=dst, in_=src)

        for dst, src in [(bom, bom_c), (lsc, ls_c), (b2c, b2_c), (s3c, s3_c),
                         (b1sb[:], b1_c), (s1sb[:], s1_c), (s2sb[:], s2_c),
                         (hs1sb[:], hs1_c), (hs2sb[:], hs2_c),
                         (gamsb[:], gam_c), (betsb[:], bet_c),
                         (sel2[:], sel2_in)]:
            plan.emit("sp", dma(dst, src), inc="ld_small", n=16)
        N_SMALL = plan.cur("ld_small")

        plan.emit("act", lambda e: e.activation(els, lsc, AF.Exp),
                  waits=[("ld_small", N_SMALL)], inc="els_done")
        plan.emit("dve", lambda e: e.tensor_mul(bels, bom, els),
                  waits=[("els_done", 1)], inc="els_done")
        plan.emit("dve", lambda e: e.reciprocal(bgsb[:], gamsb[:]),
                  waits=[("ld_small", N_SMALL)], inc="els_done")
        plan.emit("dve", lambda e: e.tensor_mul(bgsb[:], bgsb[:], betsb[:]),
                  waits=[("els_done", None)], inc="els_done")
        plan.emit("dve", lambda e: e.memset(onesb[:], 1.0), inc="els_done")
        plan.emit("dve", lambda e: e.memset(ones1[:], 1.0), inc="els_done")
        ELS_ALL = plan.cur("els_done")

        sA = ExitStack()
        with sA:
            def sb(name, shape, dtype):
                return sA.enter_context(nc.sbuf_tensor(name, shape, dtype))

            def ps(name, shape):
                return sA.enter_context(nc.psum_tensor(name, shape, F32))

            plan.section = "fold"
            sF = ExitStack()
            with sF:
                w1t_sb = [sF.enter_context(nc.sbuf_tensor(f"w1t_sb{i}", [P, 2048], F32))
                          for i in range(2)]
                w1gf = [sF.enter_context(nc.sbuf_tensor(f"w1gf{i}", [P, 2048], BF16))
                        for i in range(2)]
                combo = sF.enter_context(nc.sbuf_tensor("combo", [P, 2 * NFT], BF16))
                rowtmp2 = sF.enter_context(nc.sbuf_tensor("rowtmp2", [2, 2048], F32))
                p_r12 = sF.enter_context(nc.psum_tensor("p_r12", [2, 2048], F32))
                p_cols = sF.enter_context(nc.psum_tensor("p_cols", [P, 2 * NO1], F32))

                plan.emit("dve", lambda e: e.memset(combo[:], 1.0),
                          waits=[("els_done", ELS_ALL)], inc=None)
                for ft in range(NFT):
                    plan.emit("dve", (lambda ft: lambda e: e.tensor_copy(
                        combo[:, 2 * ft + 1:2 * ft + 2],
                        bgsb[:, ft:ft + 1]))(ft),
                        inc="els_done" if ft == NFT - 1 else None)
                ELS_ALL2 = plan.cur("els_done")
                if DBG:
                    plan.emit("sp", lambda e: e.dma_start(
                        out=combodbg, in_=combo[:]),
                        waits=[("els_done", ELS_ALL2)], inc="dbg", n=16)
                    plan.emit("sp", lambda e: e.dma_start(
                        out=bgdbg, in_=bgsb[:]), inc="dbg", n=16)

                for ft in range(NFT):
                    tsb = w1t_sb[ft % 2]
                    gsb = w1gf[ft % 2]
                    plan.emit("sp", dma(tsb[:], w1Ta[ft * P:(ft + 1) * P, :]),
                              waits=[("fold_dve", plan.lag("fold_dve", 1))],
                              inc="fold_ld", n=16)
                    plan.emit("dve",
                              (lambda tsb, gsb, ft: lambda e: e.tensor_scalar_mul(
                                  gsb[:], tsb[:], gamsb[:, ft:ft + 1]))(tsb, gsb, ft),
                              waits=[("fold_ld", None), ("els_done", ELS_ALL2),
                                     ("fold_st", plan.lag("fold_st", 16)),
                                     ("fold_pe", plan.lag("fold_pe", 1))],
                              inc="fold_dve")
                    plan.emit("act",
                              (lambda gsb, ft: lambda e: e.dma_start(
                                  out=w1g_d[:, :, ft * P:(ft + 1) * P].rearrange(
                                      "o p c -> p o c"),
                                  in_=gsb[:].rearrange("p (o c) -> p o c", o=NO1)))(
                                  gsb, ft),
                              waits=[("fold_dve", None)], inc="fold_st", n=16)
                    for j in range(4):
                        plan.emit("pe",
                                  (lambda gsb, ft, j: lambda e: e.matmul(
                                      p_r12[:, j * 512:(j + 1) * 512],
                                      combo[:, 2 * ft:2 * ft + 2],
                                      gsb[:, j * 512:(j + 1) * 512],
                                      start=(ft == 0), stop=(ft == NFT - 1)))(
                                      gsb, ft, j),
                                  waits=[("fold_dve", None)],
                                  inc="fold_pe" if j == 3 else None)
                plan.emit("dve", lambda e: e.tensor_copy(rowtmp2[:], p_r12[:]),
                          waits=[("fold_pe", None)], inc="fold_fin")
                for o in range(NO1):
                    plan.emit("pe",
                              (lambda o: lambda e: e.matmul(
                                  p_cols[:, o:o + 1], rowtmp2[0:2, o * P:(o + 1) * P],
                                  sel2[0:2, 0:1], start=True, stop=True))(o),
                              waits=[("fold_fin", 1), ("ld_small", N_SMALL)],
                              inc=None)
                for o in range(NO1):
                    plan.emit("pe",
                              (lambda o: lambda e: e.matmul(
                                  p_cols[:, NO1 + o:NO1 + o + 1],
                                  rowtmp2[0:2, o * P:(o + 1) * P],
                                  sel2[0:2, 1:2], start=True, stop=True))(o),
                              inc="fold_fin" if o == NO1 - 1 else None)
                plan.emit("dve",
                          lambda e: e.tensor_scalar_mul(nw1s[:], p_cols[:, 0:NO1], -1.0),
                          waits=[("fold_fin", 2)], inc="fold_fin")
                plan.emit("dve",
                          lambda e: e.tensor_add(bias1[:], b1sb[:],
                                                 p_cols[:, NO1:2 * NO1]),
                          inc="fold_fin")
            plan.barrier()
            FOLD_DONE = plan.cur("fold_fin")
            if DBG:
                plan.emit("sp", lambda e: e.dma_start(
                    out=rowdbg, in_=rowtmp2[:]),
                    waits=[("fold_fin", FOLD_DONE)], inc="dbg", n=16)
                plan.emit("sp", lambda e: e.dma_start(
                    out=folddbg[:, 0:NO1], in_=nw1s[:]), inc="dbg", n=16)
                plan.emit("sp", lambda e: e.dma_start(
                    out=folddbg[:, NO1:2 * NO1], in_=bias1[:]),
                    inc="dbg", n=16)
            N_FOLD_ST = NFT * 16

            precast_jobs = []
            for dd in range(ND):
                precast_jobs.append((w2Ta[:, dd * P:(dd + 1) * P], w2_d[dd], 16))
            for og in range(NH):
                precast_jobs.append((fw1T[:, og * P:(og + 1) * P], fw1_d[og], ND))
            for og in range(NH):
                precast_jobs.append((fw2T[:, og * P:(og + 1) * P], fw2_d[og], ND))
            for o2 in range(NH):
                precast_jobs.append((fh1T[:, o2 * P:(o2 + 1) * P], fh1_d[o2], NH))
            for o2 in range(NH):
                precast_jobs.append((fh2T[:, o2 * P:(o2 + 1) * P], fh2_d[o2], NH))
            for dd in range(ND):
                precast_jobs.append((fw3T[:, dd * P:(dd + 1) * P], fw3_d[dd], NH))
            n_pcjobs = len(precast_jobs)
            pc_next = [0]

            def emit_precast(count):
                if SKIP_PC:
                    return
                for _ in range(count):
                    i = pc_next[0]
                    if i >= n_pcjobs:
                        return
                    pc_next[0] += 1
                    srcj, dst, kt = precast_jobs[i]
                    plan.emit("gp",
                              (lambda srcj, dst, kt: lambda e: e.dma_start(
                                  out=dst.rearrange("p (k c) -> p k c", k=kt),
                                  in_=srcj.rearrange("(k p) c -> p k c", p=P)))(
                                  srcj, dst, kt),
                              inc="pc_st", n=16)

            for j in range(ND):
                plan.emit("gp",
                          (lambda j: lambda e: e.dma_start(
                              out=wom_d[j].rearrange("p (k c) -> p k c", k=ND),
                              in_=womT[:, j * P:(j + 1) * P].rearrange(
                                  "(k p) c -> p k c", p=P)))(j),
                          waits=([("fold_fin", FOLD_DONE)] if j == 0 else []),
                          inc="ld_wom", n=16)
            N_WOM = plan.cur("ld_wom")
            womsl = [sb(f"womsl{i}", [P, ND * P], BF16) for i in range(2)]
            w2sl = [sb(f"w2sl{i}", [P, NO1 * P], BF16) for i in range(2)]

            xTf = sb("xTf", [P, ND * BS], F32)
            xTb = sb("xTb", [P, ND * BS], BF16)
            pos_s = sb("pos_s", [P, BS], F32)
            pos_i = sb("pos_i", [P, BS], F32)
            om = [sb(f"om{i}", [P, BS], F32) for i in range(2)]
            phw = sb("phw", [P, 2 * BS], F32)
            phi = [phw[:, i * BS:(i + 1) * BS] for i in range(2)]
            tsc = [sb(f"tsc{i}", [P, 2 * BS], F32) for i in range(2)]
            rn = phw
            scb = [sb(f"sc{i}", [P, 2 * BS], F32) for i in range(2)]
            sinb = [scb[i][:, 0:BS] for i in range(2)]
            cosb = [scb[i][:, BS:2 * BS] for i in range(2)]
            cm_r = sb("cm_r", [P, BS], F32)
            cm_i = sb("cm_i", [P, BS], F32)
            S_r = [sb(f"S_r{i}", [P, BS], F32) for i in range(2)]
            S_i = [sb(f"S_i{i}", [P, BS], F32) for i in range(2)]
            uv0 = [sb(f"uv0_{i}", [P, BS], F32) for i in range(2)]
            uv1 = [sb(f"uv1_{i}", [P, BS], F32) for i in range(2)]
            sqb = [sb(f"sqb{i}", [P, BS], BF16) for i in range(2)]
            qsta = [sb(f"qsta{i}", [P, 4 * BS], BF16) for i in range(2)]
            q_st = [[qsta[i][:, q * BS:(q + 1) * BS] for i in range(2)]
                    for q in range(4)]
            ctx_dq = ctx_d.rearrange("(q k) p s -> k q p s", k=ND)
            xres = [sb(f"xres{i}", [P, BS], F32) for i in range(2)]
            muBp = [sb(f"muB{i}", [P, BS], F32) for i in range(2)]
            veB = sb("veB", [P, BS], F32)
            rstdBp = [sb(f"rstdB{i}", [P, BS], F32) for i in range(2)]
            mu2B = sb("mu2B", [P, BS], F32)
            ve2B = sb("ve2B", [P, BS], F32)
            rstd2B = sb("rstd2B", [P, BS], F32)
            ctxw = sb("ctxw", [P, NFT * BS], BF16)
            w1g_sb = [sb(f"w1g_sb{i}", [P, 16 * P], BF16) for i in range(2)]
            z1 = sb("z1", [P, BS], F32)
            z2 = sb("z2", [P, BS], F32)
            h1a = sb("h1a", [P, NO1 * BS], BF16)
            rbuf = [sb(f"rbuf{i}", [P, BS], F32) for i in range(2)]
            r_sb = sb("r_sb", [P, ND * BS], F32)
            rbf = sb("rbf0", [P, BS], BF16)
            sqr = sb("sqr0", [P, BS], BF16)
            x2fs = [sb(f"x2fs{i}", [P, BS], F32) for i in range(2)]
            x2bs = [sb(f"x2bs{i}", [P, BS], BF16) for i in range(2)]

            p_om = [ps(f"p_om{i}", [P, BS]) for i in range(2)]
            p_s1 = ps("p_s1", [P, BS])
            p_s2 = ps("p_s2", [P, BS])
            p_z = [ps(f"p_z{i}", [P, BS]) for i in range(2)]
            p_t1 = ps("p_t1", [P, BS])
            p_t2 = ps("p_t2", [P, BS])

            needq = {}

            def front_pe(b, e8, AXLB):
                ws = womsl[e8 % 2]
                plan.emit("sp", (lambda e8, ws: lambda e: e.dma_start(
                    out=ws[:], in_=wom_d[e8]))(e8, ws),
                    waits=[("ld_wom", N_WOM),
                           ("pe_om", plan.lag("pe_om", 1))],
                    inc="ld_womsl", n=16)
                LDWS = plan.cur("ld_womsl")
                pom = p_om[e8 % 2]
                for dk in range(ND):
                    plan.emit("pe",
                              (lambda pom, dk, ws: lambda e: e.matmul(
                                  pom[:],
                                  ws[:, dk * P:(dk + 1) * P],
                                  xTb[:, dk * BS:(dk + 1) * BS],
                                  start=(dk == 0), stop=(dk == ND - 1)))(
                                  pom, dk, ws),
                              waits=[("axlb", AXLB), ("ld_womsl", LDWS),
                                     ("fold_fin", FOLD_DONE),
                                     ("act_om", plan.lag("act_om", 1))],
                              inc="pe_om" if dk == ND - 1 else None)

            def front_evac(b, e8):
                pom = p_om[e8 % 2]
                plan.emit("act",
                          (lambda pom, e8: lambda e: e.activation(
                              om[e8 % 2][:], pom[:], AF.Identity,
                              bias=bels[:, e8:e8 + 1],
                              scale=els[:, e8:e8 + 1]))(pom, e8),
                          waits=[("pe_om", None), ("els_done", 2),
                                 ("s_scp", plan.lag("s_scp", 1))],
                          inc="act_om")

            def stA(b, e8, APOS):
                omb = om[e8 % 2]
                plan.emit("dve",
                          (lambda omb: lambda e: e.tensor_mul(
                              omb[:], omb[:], pos_s[:]))(omb),
                          waits=[("act_om", None), ("apos", APOS - 16)],
                          inc="dve_oms")
                init = 0.0 if b == 0 else carr_p[:, e8:e8 + 1]
                plan.emit("dve",
                          (lambda omb, init, e8: lambda e: e.tensor_tensor_scan(
                              phi[e8 % 2][:], omb[:], omb[:], init,
                              ALU.add, ALU.bypass))(omb, init, e8),
                          waits=[("dve_cphi",
                                  8 * (b - 1) + e8 + 1 if b > 0 else 0)],
                          inc="s_scp")
                plan.emit("dve",
                          (lambda e8: lambda e: e.tensor_copy(
                              carr_p[:, e8:e8 + 1], phi[e8 % 2][:, BS - 1:BS]))(e8),
                          waits=[("s_scp", None)], inc="dve_cphi")
                tw = tsc[e8 % 2]
                ph = phi[e8 % 2]
                plan.emit("dve", (lambda tw, ph: lambda e: e.tensor_scalar_mul(
                    tw[:, 0:BS], ph[:], INV_2PI))(tw, ph),
                    waits=[("act_trig", plan.lag("act_trig", 1))],
                    inc=None)
                plan.emit("dve", (lambda tw, ph: lambda e: e.tensor_scalar(
                    tw[:, BS:2 * BS], ph[:], INV_2PI, 0.25,
                    ALU.mult, ALU.add))(tw, ph),
                    inc=None)
                plan.emit("dve", (lambda tw: lambda e: e.tensor_scalar(
                    rn[:], tw[:], MAGIC, MAGIC, ALU.add, ALU.subtract))(tw),
                    inc=None)
                plan.emit("dve", (lambda tw: lambda e: e.tensor_sub(
                    tw[:], tw[:], rn[:]))(tw), inc="dve_trig")

            def stB(b, e8):
                tw = tsc[e8 % 2]
                plan.emit("act", (lambda tw, o: lambda e: e.activation(
                    o[:], tw[:], AF.Sin, bias=0.0, scale=TWO_PI))(
                    tw, scb[e8 % 2]),
                    waits=[("dve_trig", None),
                           ("dve_cm", plan.lag("dve_cm", 1)),
                           ("gp_uv", plan.lag("gp_uv", 1))],
                    inc="act_trig")

            def prevq(b, e8, back, q):
                e = e8 - back
                bb = b
                if e < 0:
                    e += ND
                    bb -= 1
                return needq.get((bb, e), {}).get(q, 0)

            def stage_quarter(b, e8, q, fnq, eng="dve", extra=()):
                stg = q_st[q][e8 % 2]
                bs = b * BS
                plan.emit(eng, fnq(stg),
                    waits=[("pe_st1", plan.lag("pe_st1", 0)),
                           ("sp_ctxst",
                            plan.lag("sp_ctxst", 0 if q == 0 else 16))]
                          + list(extra),
                    inc="dve_q")
                needq.setdefault((b, e8), {})[q] = plan.cur("dve_q")
                if q == 3:
                    qa = qsta[e8 % 2]
                    plan.emit("act", (lambda e8, qa, bs: lambda e: e.dma_start(
                        out=ctx_dq[e8][:, :, bs:bs + BS].rearrange(
                            "q p c -> p q c"),
                        in_=qa[:].rearrange("p (q c) -> p q c", q=4)))(
                        e8, qa, bs),
                        waits=[("dve_q", plan.cur("dve_q"))],
                        inc="sp_ctxst", n=16)

            def stC(b, e8, AXL):
                sin_b, cos_b = sinb[e8 % 2], cosb[e8 % 2]
                xsl = xTf[:, e8 * BS:(e8 + 1) * BS]
                plan.emit("dve", (lambda cos_b, xsl: lambda e: e.tensor_mul(
                    cm_r[:], xsl, cos_b[:]))(cos_b, xsl),
                    waits=[("act_trig", None), ("axl", AXL),
                           ("dve_q", prevq(b, e8, 1, 1))],
                    inc=None)
                plan.emit("dve", (lambda sin_b, xsl: lambda e: e.tensor_mul(
                    cm_i[:], xsl, sin_b[:]))(sin_b, xsl), inc="dve_cm")
                DCM = plan.cur("dve_cm")
                stage_quarter(b, e8, 0,
                              lambda stg: (lambda e: e.activation(
                                  stg[:], cm_r[:], AF.Identity,
                                  bias=0.0, scale=1.0)),
                              eng="act", extra=[("dve_cm", DCM)])
                stage_quarter(b, e8, 1,
                              lambda stg: (lambda e: e.activation(
                                  stg[:], cm_i[:], AF.Identity,
                                  bias=0.0, scale=1.0)),
                              eng="act", extra=[("dve_cm", DCM)])
                plan.emit("dve", (lambda Sb, init: lambda e: e.tensor_tensor_scan(
                    Sb[:], cm_r[:], cm_r[:], init, ALU.add, ALU.bypass))(
                    S_r[e8 % 2], 0.0 if b == 0 else carr_r[:, e8:e8 + 1]),
                    waits=[("dve_cr", 8 * (b - 1) + e8 + 1 if b > 0 else 0),
                           ("gp_uv", plan.lag("gp_uv", 2))],
                    inc="s_scr")
                plan.emit("dve", (lambda Sb, e8: lambda e: e.tensor_copy(
                    carr_r[:, e8:e8 + 1], Sb[:, BS - 1:BS]))(S_r[e8 % 2], e8),
                    waits=[("s_scr", None)], inc="dve_cr")
                plan.emit("dve", (lambda Sb, init: lambda e: e.tensor_tensor_scan(
                    Sb[:], cm_i[:], cm_i[:], init, ALU.add, ALU.bypass))(
                    S_i[e8 % 2], 0.0 if b == 0 else carr_i[:, e8:e8 + 1]),
                    waits=[("dve_ci", 8 * (b - 1) + e8 + 1 if b > 0 else 0),
                           ("gp_uv", plan.lag("gp_uv", 2))],
                    inc="s_sci")
                plan.emit("dve", (lambda Sb, e8: lambda e: e.tensor_copy(
                    carr_i[:, e8:e8 + 1], Sb[:, BS - 1:BS]))(S_i[e8 % 2], e8),
                    waits=[("s_sci", None)], inc="dve_ci")

            def stD(b, e8):
                Srb, Sib = S_r[e8 % 2], S_i[e8 % 2]
                sin_b, cos_b = sinb[e8 % 2], cosb[e8 % 2]
                plan.emit("dve", (lambda Srb, cos_b, e8: lambda e: e.tensor_mul(
                    uv0[e8 % 2][:], Srb[:], cos_b[:]))(Srb, cos_b, e8),
                    waits=[("dve_cr", None), ("dve_ci", None),
                           ("dve_q", prevq(b, e8, 2, 3))],
                    inc=None)
                plan.emit("dve", (lambda Sib, sin_b, e8: lambda e: e.tensor_mul(
                    uv1[e8 % 2][:], Sib[:], sin_b[:]))(Sib, sin_b, e8),
                    inc="gp_uv")

            def stE(b, e8, APOS):
                u0, u1b = uv0[e8 % 2], uv1[e8 % 2]
                plan.emit("dve", (lambda u0, u1b: lambda e: e.tensor_add(
                    u0[:], u0[:], u1b[:]))(u0, u1b),
                    waits=[("gp_uv", None)], inc=None)
                stage_quarter(b, e8, 2,
                              (lambda u0: (lambda stg: (lambda e: e.tensor_mul(
                                  stg[:], u0[:], pos_i[:]))))(u0),
                              extra=[("apos", APOS)])

            def stF(b, e8):
                Srb, Sib = S_r[e8 % 2], S_i[e8 % 2]
                sin_b, cos_b = sinb[e8 % 2], cosb[e8 % 2]
                plan.emit("dve", (lambda Sib, cos_b, e8: lambda e: e.tensor_mul(
                    uv0[e8 % 2][:], Sib[:], cos_b[:]))(Sib, cos_b, e8),
                    waits=[("dve_q", needq[(b, e8)][2])], inc=None)
                plan.emit("dve", (lambda Srb, sin_b, e8: lambda e: e.tensor_mul(
                    uv1[e8 % 2][:], Srb[:], sin_b[:]))(Srb, sin_b, e8),
                    inc="gp_uv")

            def stG(b, e8):
                u0, u1b = uv0[e8 % 2], uv1[e8 % 2]
                plan.emit("dve", (lambda u0, u1b: lambda e: e.tensor_sub(
                    u0[:], u0[:], u1b[:]))(u0, u1b),
                    waits=[("gp_uv", None)], inc=None)
                stage_quarter(b, e8, 3,
                              (lambda u0: (lambda stg: (lambda e: e.tensor_mul(
                                  stg[:], u0[:], pos_i[:]))))(u0))

            def front_stats(b, e8):
                for q in range(4):
                    stg = q_st[q][e8 % 2]
                    sq = sqb[q % 2]
                    NQ = needq[(b, e8)][q]
                    plan.emit("act", (lambda stg, sq: lambda e: e.activation(
                        sq[:], stg[:], AF.Square))(stg, sq),
                        waits=[("dve_q", NQ),
                               ("pe_st1", plan.lag("pe_st1", 1))],
                        inc="act_sq")
                    first = (e8 == 0 and q == 0)
                    last = (e8 == ND - 1 and q == 3)
                    plan.emit("pe", (lambda stg, first, last: lambda e: e.matmul(
                        p_s1[:], onesb[:], stg[:], start=first, stop=last))(
                        stg, first, last),
                        waits=[("dve_q", NQ), ("els_done", ELS_ALL),
                               ("dve_var1", plan.lag("dve_var1", 0))],
                        inc=None)
                    plan.emit("pe", (lambda sq, first, last: lambda e: e.matmul(
                        p_s2[:], onesb[:], sq[:], start=first, stop=last))(
                        sq, first, last),
                        waits=[("act_sq", None)],
                        inc="pe_st1")

            def emit_ln1(b):
                muB, rstdB = muBp[b % 2], rstdBp[b % 2]
                PE_ST1 = plan.cur("pe_st1")
                plan.emit("dve",
                          (lambda muB: lambda e: e.tensor_scalar_mul(
                              muB[:], p_s1[:], 1.0 / 4096.0))(muB),
                          waits=[("pe_st1", PE_ST1),
                                 ("act_sqrt1", plan.lag("act_sqrt1", 0)),
                                 ("gp_z", plan.lag("gp_z", 16))],
                          inc=None)
                plan.emit("dve", (lambda muB: lambda e: e.tensor_mul(
                    veB[:], muB[:], muB[:]))(muB), inc=None)
                plan.emit("dve", lambda e: e.tensor_scalar(
                    veB[:], veB[:], EPS, -1.0, ALU.subtract, ALU.mult), inc=None)
                plan.emit("dve", lambda e: e.scalar_tensor_tensor(
                    veB[:], p_s2[:], 1.0 / 4096.0, veB[:], ALU.mult, ALU.add),
                    inc="dve_var1")
                plan.emit("act", (lambda rstdB: lambda e: e.activation(
                    rstdB[:], veB[:], AF.Sqrt))(rstdB),
                    waits=[("dve_var1", None), ("gp_z", plan.lag("gp_z", 16))],
                    inc="act_sqrt1")
                plan.emit("dve", (lambda rstdB: lambda e: e.reciprocal(
                    rstdB[:], rstdB[:]))(rstdB),
                    waits=[("act_sqrt1", None)], inc="dve_rstd1")

            def make_tail(b):
                bs = b * BS
                muB, rstdB = muBp[b % 2], rstdBp[b % 2]
                SPST_B = plan.cur("sp_ctxst")
                RSTD1 = plan.cur("dve_rstd1")
                PEW1_0 = plan.cur("pe_w1")
                PEW2_0 = plan.cur("pe_w2")
                cell = {}
                groups = []

                def reload_group():
                    for f4 in range(NFT // 4):
                        plan.emit("sp", (lambda f4, bs: lambda e: e.dma_start(
                            out=ctxw[:, 4 * f4 * BS:(4 * f4 + 4) * BS].rearrange(
                                "p (f c) -> p f c", f=4),
                            in_=ctx_d[4 * f4:4 * f4 + 4, :, bs:bs + BS].rearrange(
                                "f p c -> p f c")))(f4, bs),
                            waits=[("sp_ctxst", SPST_B),
                                   ("pe_w1", plan.lag("pe_w1", 0))],
                            inc="sp_ctxw", n=16)
                    cell["ctxw"] = plan.cur("sp_ctxw")
                groups.append(reload_group)

                def w1_mm(o):
                    spw = []
                    for hh in range(2):
                        wsb = w1g_sb[hh]
                        plan.emit("sp", (lambda o, hh, wsb: lambda e: e.dma_start(
                            out=wsb[:],
                            in_=w1g_d[o, :, hh * 16 * P:(hh + 1) * 16 * P]))(
                            o, hh, wsb),
                            waits=[("fold_st", N_FOLD_ST),
                                   ("pe_w1", plan.lag("pe_w1", 1 - hh))],
                            inc="sp_w1g", n=16)
                        spw.append(plan.cur("sp_w1g"))
                    pz = p_z[o % 2]
                    for ft in range(NFT):
                        hh = ft // 16
                        plan.emit("pe", (lambda pz, hh, ft: lambda e: e.matmul(
                            pz[:], w1g_sb[hh][:, (ft % 16) * P:(ft % 16 + 1) * P],
                            ctxw[:, ft * BS:(ft + 1) * BS],
                            start=(ft == 0), stop=(ft == NFT - 1)))(pz, hh, ft),
                            waits=[("sp_ctxw", cell["ctxw"]), ("sp_w1g", spw[hh]),
                                   ("act_rb", plan.lag("act_rb", 0)),
                                   ("dve_zc", plan.lag("dve_zc", 0))],
                            inc="pe_w1" if ft in (15, NFT - 1) else None)

                def w1_post(o):
                    pz = p_z[o % 2]
                    plan.emit("dve",
                              (lambda pz, o: lambda e: e.scalar_tensor_tensor(
                                  z1[:], muB[:], nw1s[:, o:o + 1], pz[:],
                                  ALU.mult, ALU.add))(pz, o),
                              waits=[("pe_w1", PEW1_0 + 2 * (o + 1)),
                                     ("fold_fin", FOLD_DONE),
                                     ("gp_z", plan.lag("gp_z", 0))],
                              inc="dve_zc")
                    plan.emit("dve", (lambda rstdB: lambda e: e.tensor_mul(
                        z2[:], z1[:], rstdB[:]))(rstdB),
                        waits=[("dve_zc", None), ("dve_rstd1", RSTD1),
                               ("act_h1a", plan.lag("act_h1a", 0))],
                        inc="gp_z")
                    plan.emit("act", (lambda o: lambda e: e.activation(
                        h1a[:, o * BS:(o + 1) * BS], z2[:], AF.Gelu,
                        bias=bias1[:, o:o + 1], scale=1.0))(o),
                        waits=[("gp_z", None),
                               ("pe_w2", plan.lag("pe_w2", 0))],
                        inc="act_h1a")

                for o in range(NO1):
                    if o == 0:
                        groups.append((lambda o: lambda: w1_mm(o))(o))
                    else:
                        groups.append((lambda o: lambda: (w1_mm(o),
                                                          w1_post(o - 1)))(o))
                groups.append(lambda: w1_post(NO1 - 1))

                def w2_mm(dd):
                    wsl2 = w2sl[dd % 2]
                    plan.emit("sp", (lambda dd, wsl2: lambda e: e.dma_start(
                        out=wsl2[:], in_=w2_d[dd]))(dd, wsl2),
                        waits=[("pe_w2", plan.lag("pe_w2", 1)),
                               ("pc_st", min(16 * (dd + 1), plan.cur("pc_st")))],
                        inc="ld_w2", n=16)
                    cell[("ldw2", dd)] = plan.cur("ld_w2")
                    xr = xres[dd % 2]
                    plan.emit("sp", (lambda dd, bs, xr: lambda e: e.dma_start(
                        out=xr[:], in_=xT[dd * P:(dd + 1) * P, bs:bs + BS]))(
                        dd, bs, xr),
                        waits=[("dve_r", plan.lag("dve_r", 0))],
                        inc="axr", n=16)
                    cell[("axr", dd)] = plan.cur("axr")
                    pr = p_z[dd % 2]
                    for ok in range(NO1):
                        plan.emit("pe", (lambda pr, wsl2, ok: lambda e: e.matmul(
                            pr[:], wsl2[:, ok * P:(ok + 1) * P],
                            h1a[:, ok * BS:(ok + 1) * BS],
                            start=(ok == 0), stop=(ok == NO1 - 1)))(pr, wsl2, ok),
                            waits=[("act_h1a", plan.cur("act_h1a")),
                                   ("ld_w2", cell[("ldw2", dd)]),
                                   ("dve_zc", plan.lag("dve_zc", 0)),
                                   ("act_rb", plan.lag("act_rb", 1))],
                            inc="pe_w2" if ok == NO1 - 1 else None)

                def w2_post(dd):
                    pr = p_z[dd % 2]
                    plan.emit("act", (lambda pr, dd, o: lambda e: e.activation(
                        o[:], pr[:], AF.Identity, bias=b2c[:, dd:dd + 1],
                        scale=1.0))(pr, dd, rbuf[dd % 2]),
                        waits=[("pe_w2", PEW2_0 + dd + 1),
                               ("dve_r", plan.lag("dve_r", 1))],
                        inc="act_rb")
                    xr = xres[dd % 2]
                    plan.emit("dve",
                              (lambda xr, dd, o: lambda e: e.scalar_tensor_tensor(
                                  r_sb[:, dd * BS:(dd + 1) * BS], xr[:], 2.0, o[:],
                                  ALU.mult, ALU.add))(xr, dd, rbuf[dd % 2]),
                              waits=[("act_rb", None), ("axr", cell[("axr", dd)]),
                                     ("dve_x2", plan.lag("dve_x2", 0))],
                              inc=None)
                    plan.emit("dve", (lambda dd: lambda e: e.tensor_copy(
                        rbf[:], r_sb[:, dd * BS:(dd + 1) * BS]))(dd),
                        waits=[("pe_st2", plan.lag("pe_st2", 0))],
                        inc="dve_r")
                    plan.emit("act", lambda e: e.activation(
                        sqr[:], rbf[:], AF.Square),
                        waits=[("dve_r", None),
                               ("pe_st2", plan.lag("pe_st2", 0))],
                        inc="act_sqr")
                    plan.emit("pe", (lambda dd: lambda e: e.matmul(
                        p_t1[:], onesb[:], rbf[:], start=(dd == 0),
                        stop=(dd == ND - 1)))(dd),
                        waits=[("dve_r", None),
                               ("dve_var2", plan.lag("dve_var2", 0))],
                        inc=None)
                    plan.emit("pe", (lambda dd: lambda e: e.matmul(
                        p_t2[:], onesb[:], sqr[:], start=(dd == 0),
                        stop=(dd == ND - 1)))(dd),
                        waits=[("act_sqr", None)],
                        inc="pe_st2")

                for dd in range(ND):
                    if dd == 0:
                        groups.append((lambda dd: lambda: w2_mm(dd))(dd))
                    else:
                        groups.append((lambda dd: lambda: (w2_mm(dd),
                                                           w2_post(dd - 1)))(dd))
                groups.append(lambda: w2_post(ND - 1))

                def ln2_group():
                    PE_ST2 = plan.cur("pe_st2")
                    plan.emit("dve",
                              lambda e: e.tensor_scalar_mul(mu2B[:], p_t1[:],
                                                            1.0 / 1024.0),
                              waits=[("pe_st2", PE_ST2)], inc=None)
                    plan.emit("dve", lambda e: e.tensor_mul(ve2B[:], mu2B[:], mu2B[:]),
                              inc=None)
                    plan.emit("dve", lambda e: e.tensor_scalar(
                        ve2B[:], ve2B[:], EPS, -1.0, ALU.subtract, ALU.mult),
                        inc=None)
                    plan.emit("dve", lambda e: e.scalar_tensor_tensor(
                        ve2B[:], p_t2[:], 1.0 / 1024.0, ve2B[:], ALU.mult, ALU.add),
                        inc="dve_var2")
                    plan.emit("act", lambda e: e.activation(rstd2B[:], ve2B[:],
                                                            AF.Sqrt),
                              waits=[("dve_var2", None)], inc="act_sqrt2")
                    plan.emit("dve", lambda e: e.reciprocal(rstd2B[:], rstd2B[:]),
                              waits=[("act_sqrt2", None)], inc="dve_rstd2")
                groups.append(ln2_group)

                def apply_group(dd):
                    xf = x2fs[dd % 2]
                    xb = x2bs[dd % 2]
                    plan.emit("dve", (lambda dd, xf: lambda e: e.tensor_sub(
                        xf[:], r_sb[:, dd * BS:(dd + 1) * BS], mu2B[:]))(dd, xf),
                        waits=[("dve_rstd2", None),
                               ("sp_x2st", plan.lag("sp_x2st", 32))],
                        inc=None)
                    plan.emit("dve", (lambda xf: lambda e: e.tensor_mul(
                        xf[:], xf[:], rstd2B[:]))(xf), inc=None)
                    plan.emit("act", (lambda xf, xb: lambda e: e.activation(
                        xb[:], xf[:], AF.Identity, bias=0.0, scale=1.0))(xf, xb),
                        waits=[("ch_dve", plan.cur("ch_dve"))], inc="dve_x2")
                    plan.emit("act", (lambda dd, xf, bs: lambda e: e.dma_start(
                        out=x2f_d[dd, :, bs:bs + BS], in_=xf[:]))(dd, xf, bs),
                        waits=[("dve_x2", None)], inc="sp_x2st", n=16)
                    plan.emit("act", (lambda dd, xb, bs: lambda e: e.dma_start(
                        out=x2b_d[dd, :, bs:bs + BS], in_=xb[:]))(dd, xb, bs),
                        inc="sp_x2st", n=16)
                for dd in range(ND):
                    groups.append((lambda dd: lambda: apply_group(dd))(dd))
                return groups

            pending = []
            for b in range(0 if not SKIP_ATT else NB, NB):
                bs = b * BS
                plan.section = "front"
                plan.emit("sp",
                          (lambda bs: lambda e: e.dma_start(
                              out=xTf[:].rearrange("p (k s) -> p k s", k=ND),
                              in_=xT[:, bs:bs + BS].rearrange(
                                  "(k p) s -> p k s", p=P)))(bs),
                          waits=[("dve_cm", plan.lag("dve_cm", 0)),
                                 ("axlb", plan.lag("axlb", 0)),
                                 ("fold_fin", FOLD_DONE)],
                          inc="axl", n=16)
                plan.emit("sp", (lambda bs: lambda e: e.dma_start(
                    out=pos_s[:], in_=posscale[:, bs:bs + BS]))(bs),
                    waits=[("dve_oms", plan.lag("dve_oms", 0))], inc="apos", n=16)
                plan.emit("sp", (lambda bs: lambda e: e.dma_start(
                    out=pos_i[:], in_=posinv[:, bs:bs + BS]))(bs),
                    waits=[("dve_q", plan.lag("dve_q", 0))], inc="apos", n=16)
                AXL = plan.cur("axl")
                plan.emit("dve", lambda e: e.tensor_copy(xTb[:], xTf[:]),
                          waits=[("axl", AXL),
                                 ("pe_om", plan.lag("pe_om", 0))],
                          inc="axlb")
                AXLB = plan.cur("axlb")
                APOS = plan.cur("apos")

                emit_precast(PC_PER_BLOCK)

                gi = 0
                NIT = ND + 2
                for i in range(NIT):
                    take = (len(pending) * (i + 1) + NIT - 1) // NIT
                    while gi < min(take, len(pending)):
                        pending[gi]()
                        gi += 1
                    if i < ND:
                        front_pe(b, i, AXLB)
                    if i >= 1 and i - 1 < ND:
                        stB(b, i - 1)
                    if i < ND:
                        front_evac(b, i)
                    if i >= 3 and i - 3 < ND:
                        front_stats(b, i - 3)
                    if i >= 2 and i - 2 < ND:
                        stE(b, i - 2, APOS)
                        stF(b, i - 2)
                    if i < ND:
                        stA(b, i, APOS)
                    if i >= 1 and i - 1 < ND:
                        stC(b, i - 1, AXL)
                    if i >= 2 and i - 2 < ND:
                        stG(b, i - 2)
                    if i >= 1 and i - 1 < ND:
                        stD(b, i - 1)
                while gi < len(pending):
                    pending[gi]()
                    gi += 1
                front_stats(b, ND - 1)
                plan.section = "ln1"
                emit_ln1(b)
                pending = make_tail(b)
            for g in pending:
                g()

            emit_precast(n_pcjobs)
            N_PC = plan.cur("pc_st")
            N_X2 = plan.cur("sp_x2st")
            plan.barrier()

        sB = ExitStack()
        with sB:
            def sb2(name, shape, dtype):
                return sB.enter_context(nc.sbuf_tensor(name, shape, dtype))

            def ps2(name, shape):
                return sB.enter_context(nc.psum_tensor(name, shape, F32))

            x2b_sb = [sb2(f"x2b_sb{i}", [P, ND * FBS], BF16) for i in range(2)]
            wsl = [sb2(f"wsl{i}", [P, ND * P], BF16) for i in range(6)]
            hsl = [sb2(f"hsl{i}", [P, NH * P], BF16) for i in range(3)]
            t1b = [sb2(f"t1b{i}", [P, FBS], F32) for i in range(2)]
            g_sb = [sb2(f"g_sb{i}", [P, NH * FBS], BF16) for i in range(2)]
            hb_sb = sb2("hb_sb", [P, NH * FBS], BF16)
            x2f_sl = [sb2(f"x2f_sl{i}", [P, FBS], F32) for i in range(2)]
            r3b = [sb2(f"r3b{i}", [P, FBS], BF16) for i in range(2)]
            sq3 = [sb2(f"sq3{i}", [P, FBS], BF16) for i in range(2)]
            r3s = [sb2(f"r3s{i}", [P, FBS], F32) for i in range(2)]
            mu3B = sb2("mu3B", [P, FBS], F32)
            ve3B = sb2("ve3B", [P, FBS], F32)
            rstd3B = sb2("rstd3B", [P, FBS], F32)
            r3l = [sb2(f"r3l{i}", [P, FBS], F32) for i in range(2)]
            out_s = [sb2(f"out_s{i}", [P, FBS], F32) for i in range(2)]

            f_z1 = [ps2(f"f_z1{i}", [P, FBS]) for i in range(2)]
            f_z2 = [ps2(f"f_z2{i}", [P, FBS]) for i in range(2)]
            f_ph = [ps2(f"f_ph{i}", [P, FBS]) for i in range(2)]
            f_p4 = [ps2(f"f_p4{i}", [P, FBS]) for i in range(2)]

            for fb in range(0 if not SKIP_FFN else NFB, NFB):
                bs = fb * FBS
                plan.section = "ffn1"
                x2b = x2b_sb[fb % 2]
                gb = g_sb[fb % 2]
                plan.emit("sp", (lambda bs, x2b: lambda e: e.dma_start(
                    out=x2b[:].rearrange("p (k c) -> p k c", k=ND),
                    in_=x2b_d[:, :, bs:bs + FBS].rearrange("k p c -> p k c")))(
                    bs, x2b),
                    waits=[("sp_x2st", N_X2),
                           ("f_pe1", plan.lag("f_pe1", NH))],
                    inc="f_x2l", n=16)
                F_X2L = plan.cur("f_x2l")
                for og in range(NH):
                    w1s, w2s = wsl[2 * (og % 3)], wsl[2 * (og % 3) + 1]
                    plan.emit("sp", (lambda og, w1s: lambda e: e.dma_start(
                        out=w1s[:], in_=fw1_d[og]))(og, w1s),
                        waits=[("pc_st", min(N_PC, plan.cur("pc_st"))),
                               ("f_pe1", plan.lag("f_pe1", 2))],
                        inc="f_wl", n=16)
                    plan.emit("sp", (lambda og, w2s: lambda e: e.dma_start(
                        out=w2s[:], in_=fw2_d[og]))(og, w2s),
                        inc="f_wl", n=16)
                    pz1, pz2 = f_z1[og % 2], f_z2[og % 2]
                    w1s, w2s = wsl[2 * (og % 3)], wsl[2 * (og % 3) + 1]
                    for dk in range(ND):
                        plan.emit("pe", (lambda pz1, w1s, dk, x2b: lambda e: e.matmul(
                            pz1[:], w1s[:, dk * P:(dk + 1) * P],
                            x2b[:, dk * FBS:(dk + 1) * FBS],
                            start=(dk == 0), stop=(dk == ND - 1)))(pz1, w1s, dk, x2b),
                            waits=[("f_x2l", F_X2L), ("f_wl", None),
                                   ("f_act1", plan.lag("f_act1", 1))],
                            inc=None)
                    for dk in range(ND):
                        plan.emit("pe", (lambda pz2, w2s, dk, x2b: lambda e: e.matmul(
                            pz2[:], w2s[:, dk * P:(dk + 1) * P],
                            x2b[:, dk * FBS:(dk + 1) * FBS],
                            start=(dk == 0), stop=(dk == ND - 1)))(pz2, w2s, dk, x2b),
                            waits=[("f_dve1", plan.lag("f_dve1", 1))],
                            inc="f_pe1" if dk == ND - 1 else None)
                    t1 = t1b[og % 2]
                    plan.emit("act", (lambda pz1, og, t1: lambda e: e.activation(
                        t1[:], pz1[:], AF.Silu, bias=0.0,
                        scale=s1sb[:, og:og + 1]))(pz1, og, t1),
                        waits=[("f_pe1", None), ("ld_small", N_SMALL)],
                        inc="f_act1")
                    plan.emit("dve",
                              (lambda pz2, og, t1, gb: lambda e: e.scalar_tensor_tensor(
                                  gb[:, og * FBS:(og + 1) * FBS], pz2[:],
                                  s2sb[:, og:og + 1], t1[:],
                                  ALU.mult, ALU.mult))(pz2, og, t1, gb),
                              waits=[("f_act1", None),
                                     ("f_pe4", plan.lag("f_pe4", ND))],
                              inc="f_dve1")
                F_DVE1 = plan.cur("f_dve1")
                plan.section = "ffn2"
                for o2 in range(NH):
                    hs = hsl[o2 % 3]
                    plan.emit("sp", (lambda o2, hs: lambda e: e.dma_start(
                        out=hs[:], in_=fh1_d[o2]))(o2, hs),
                        waits=[("pc_st", min(N_PC, plan.cur("pc_st"))),
                               ("f_pe2", plan.lag("f_pe2", 2))]
                              + ([("f_pe4", plan.lag("f_pe4", (1 - o2) % 3))]
                                 if o2 < 3 else []),
                        inc="f_wl", n=16)
                    ph = f_ph[o2 % 2]
                    if o2 < 2:
                        pass
                    for kt in range(NH):
                        plan.emit("pe", (lambda ph, hs, kt, gb: lambda e: e.matmul(
                            ph[:], hs[:, kt * P:(kt + 1) * P],
                            gb[:, kt * FBS:(kt + 1) * FBS],
                            start=(kt == 0), stop=(kt == NH - 1)))(ph, hs, kt, gb),
                            waits=[("f_dve1", F_DVE1), ("f_wl", None),
                                   ("f_var3", plan.lag("f_var3", 0)),
                                   ("f_act2", plan.lag("f_act2", 1))],
                            inc="f_pe2" if kt == NH - 1 else None)
                    plan.emit("act", (lambda ph, o2: lambda e: e.activation(
                        hb_sb[:, o2 * FBS:(o2 + 1) * FBS], ph[:], AF.Gelu,
                        bias=0.0, scale=hs1sb[:, o2:o2 + 1]))(ph, o2),
                        waits=[("f_pe2", None)], inc="f_act2")
                F_ACT2 = plan.cur("f_act2")
                plan.section = "ffn3"
                for o3 in range(NH):
                    hs = hsl[o3 % 3]
                    plan.emit("sp", (lambda o3, hs: lambda e: e.dma_start(
                        out=hs[:], in_=fh2_d[o3]))(o3, hs),
                        waits=[("f_pe3", plan.lag("f_pe3", 2))]
                              + ([("f_pe2", plan.lag("f_pe2", (1 - o3) % 3))]
                                 if o3 < 3 else []),
                        inc="f_wl", n=16)
                    ph = f_ph[o3 % 2]
                    for kt in range(NH):
                        plan.emit("pe", (lambda ph, hs, kt: lambda e: e.matmul(
                            ph[:], hs[:, kt * P:(kt + 1) * P],
                            hb_sb[:, kt * FBS:(kt + 1) * FBS],
                            start=(kt == 0), stop=(kt == NH - 1)))(ph, hs, kt),
                            waits=[("f_act2", F_ACT2), ("f_wl", None),
                                   ("f_act3", plan.lag("f_act3", 1))],
                            inc="f_pe3" if kt == NH - 1 else None)
                    plan.emit("act", (lambda ph, o3, gb: lambda e: e.activation(
                        gb[:, o3 * FBS:(o3 + 1) * FBS], ph[:], AF.Gelu,
                        bias=0.0, scale=hs2sb[:, o3:o3 + 1]))(ph, o3, gb),
                        waits=[("f_pe3", None)], inc="f_act3")
                F_ACT3 = plan.cur("f_act3")
                plan.section = "ffn4"
                for dd in range(ND):
                    hs = hsl[dd % 3]
                    plan.emit("sp", (lambda dd, hs: lambda e: e.dma_start(
                        out=hs[:], in_=fw3_d[dd]))(dd, hs),
                        waits=[("f_pe4", plan.lag("f_pe4", 2))]
                              + ([("f_pe3", plan.lag("f_pe3", (1 - dd) % 3))]
                                 if dd < 3 else []),
                        inc="f_wl", n=16)
                    plan.emit("gp", (lambda dd, bs, x: lambda e: e.dma_start(
                        out=x[:], in_=x2f_d[dd, :, bs:bs + FBS]))(
                        dd, bs, x2f_sl[dd % 2]),
                        waits=[("f_dve4", plan.lag("f_dve4", 1))],
                        inc="f_x2fl", n=16)
                    p4 = f_p4[dd % 2]
                    for kt in range(NH):
                        plan.emit("pe", (lambda p4, hs, kt, gb: lambda e: e.matmul(
                            p4[:], hs[:, kt * P:(kt + 1) * P],
                            gb[:, kt * FBS:(kt + 1) * FBS],
                            start=(kt == 0), stop=(kt == NH - 1)))(p4, hs, kt, gb),
                            waits=[("f_act3", F_ACT3), ("f_wl", None),
                                   ("f_dve4", plan.lag("f_dve4", 1))],
                            inc="f_pe4" if kt == NH - 1 else None)
                    r3 = r3s[dd % 2]
                    plan.emit("dve",
                              (lambda p4, dd, r3, x: lambda e: e.scalar_tensor_tensor(
                                  r3[:], p4[:], s3c[:, dd:dd + 1], x[:],
                                  ALU.mult, ALU.add))(p4, dd, r3, x2f_sl[dd % 2]),
                              waits=[("f_pe4", None), ("f_x2fl", None),
                                     ("f_r3st", plan.lag("f_r3st", 32)),
                                     ("ld_small", N_SMALL)],
                              inc=None)
                    plan.emit("dve", (lambda r3, dd: lambda e: e.tensor_copy(
                        r3b[dd % 2][:], r3[:]))(r3, dd),
                        waits=[("f_st3", plan.lag("f_st3", 1))],
                        inc="f_dve4")
                    plan.emit("act", (lambda r3, dd, bs: lambda e: e.dma_start(
                        out=r3_d[dd, :, bs:bs + FBS], in_=r3[:]))(r3, dd, bs),
                        waits=[("f_dve4", None)], inc="f_r3st", n=16)
                    plan.emit("act", (lambda dd: lambda e: e.activation(
                        sq3[dd % 2][:], r3b[dd % 2][:], AF.Square))(dd),
                        waits=[("f_dve4", None),
                               ("f_st3", plan.lag("f_st3", 1))],
                        inc="f_sq3")
                    plan.emit("pe", (lambda dd: lambda e: e.matmul(
                        f_ph[0][:], onesb[:], r3b[dd % 2][:],
                        start=(dd == 0), stop=(dd == ND - 1)))(dd),
                        waits=[("f_dve4", None),
                               ("f_var3", plan.lag("f_var3", 0))],
                        inc=None)
                    plan.emit("pe", (lambda dd: lambda e: e.matmul(
                        f_ph[1][:], onesb[:], sq3[dd % 2][:],
                        start=(dd == 0), stop=(dd == ND - 1)))(dd),
                        waits=[("f_sq3", None)],
                        inc="f_st3")
                plan.section = "ffnln"
                F_ST3 = plan.cur("f_st3")
                plan.emit("dve",
                          lambda e: e.tensor_scalar_mul(mu3B[:], f_ph[0][:],
                                                        1.0 / 1024.0),
                          waits=[("f_st3", F_ST3)], inc=None)
                plan.emit("dve", lambda e: e.tensor_mul(ve3B[:], mu3B[:], mu3B[:]),
                          inc=None)
                plan.emit("dve", lambda e: e.tensor_scalar(
                    ve3B[:], ve3B[:], EPS, -1.0, ALU.subtract, ALU.mult), inc=None)
                plan.emit("dve", lambda e: e.scalar_tensor_tensor(
                    ve3B[:], f_ph[1][:], 1.0 / 1024.0, ve3B[:], ALU.mult, ALU.add),
                    inc="f_var3")
                plan.emit("act", lambda e: e.activation(rstd3B[:], ve3B[:], AF.Sqrt),
                          waits=[("f_var3", None)], inc="f_sqrt3")
                plan.emit("dve", lambda e: e.reciprocal(rstd3B[:], rstd3B[:]),
                          waits=[("f_sqrt3", None)], inc="f_rstd3")
                for dd in range(ND):
                    rl = r3l[dd % 2]
                    plan.emit("gp", (lambda dd, bs, rl: lambda e: e.dma_start(
                        out=rl[:], in_=r3_d[dd, :, bs:bs + FBS]))(dd, bs, rl),
                        waits=[("f_r3st", plan.cur("f_r3st") - 16 * (ND - 1 - dd)),
                               ("f_out", plan.lag("f_out", 1))],
                        inc="f_r3l", n=16)
                    ob = out_s[dd % 2]
                    plan.emit("dve", (lambda rl, ob: lambda e: e.tensor_sub(
                        ob[:], rl[:], mu3B[:]))(rl, ob),
                        waits=[("f_rstd3", None), ("f_r3l", None),
                               ("f_outst", plan.lag("f_outst", 16))],
                        inc=None)
                    plan.emit("dve", (lambda ob: lambda e: e.tensor_mul(
                        ob[:], ob[:], rstd3B[:]))(ob), inc="f_out")
                    plan.emit("act", (lambda dd, bs, ob: lambda e: e.dma_start(
                        out=outT[dd, :, bs:bs + FBS], in_=ob[:]))(dd, bs, ob),
                        waits=[("f_out", None)], inc="f_outst", n=16)

        N_OUT = plan.cur("f_outst")
        plan.emit("sp", lambda e: None, waits=[("f_outst", N_OUT)])

        with nc.Block() as block:
            @block.sync
            def _(eng):
                plan.replay("sp", eng, sems)

            @block.gpsimd
            def _(eng):
                plan.replay("gp", eng, sems)

            @block.vector
            def _(eng):
                plan.replay("dve", eng, sems)

            @block.scalar
            def _(eng):
                plan.replay("act", eng, sems)

            @block.tensor
            def _(eng):
                plan.replay("pe", eng, sems)

    return nc


def _prep_inputs(inputs):
    f32 = np.float32
    asc = np.ascontiguousarray

    def colsplit(v, ntiles):
        return asc(np.asarray(v, f32).reshape(ntiles, P).T)

    x = np.asarray(inputs["L0_attn_x"], dtype=f32)
    shared = dict(
        womT=asc(np.asarray(inputs["L0_attn_W_omega"], f32).T),
        w1Ta=asc(np.asarray(inputs["L0_attn_W1"], f32).T),
        w2Ta=asc(np.asarray(inputs["L0_attn_W2"], f32).T),
        fw1T=asc(np.asarray(inputs["L0_ffn_w1_w"], f32).T),
        fw2T=asc(np.asarray(inputs["L0_ffn_w2_w"], f32).T),
        fh1T=asc(np.asarray(inputs["L0_ffn_h1_w"], f32).T),
        fh2T=asc(np.asarray(inputs["L0_ffn_h2_w"], f32).T),
        fw3T=asc(np.asarray(inputs["L0_ffn_w3_w"], f32).T),
        bom_c=colsplit(inputs["L0_attn_b_omega"], ND),
        ls_c=colsplit(inputs["L0_attn_log_scale"], ND),
        gam_c=colsplit(inputs["L0_attn_ln_gamma"], NFT),
        bet_c=colsplit(inputs["L0_attn_ln_beta"], NFT),
        b1_c=colsplit(inputs["L0_attn_b1"], NO1),
        b2_c=colsplit(inputs["L0_attn_b2"], ND),
        s1_c=colsplit(inputs["L0_ffn_w1_s"], NH),
        s2_c=colsplit(inputs["L0_ffn_w2_s"], NH),
        hs1_c=colsplit(inputs["L0_ffn_h1_s"], NH),
        hs2_c=colsplit(inputs["L0_ffn_h2_s"], NH),
        s3_c=colsplit(inputs["L0_ffn_w3_s"], ND),
    )
    shared["sel2_in"] = np.eye(2, dtype=f32)
    pos = np.arange(1, S + 1, dtype=np.float64)
    shared["posscale"] = asc(np.broadcast_to(
        (1.0 / np.sqrt(pos)).astype(f32), (P, S)))
    shared["posinv"] = asc(np.broadcast_to(
        (1.0 / pos).astype(f32), (P, S)))

    in_maps = []
    for i in range(8):
        m = dict(shared)
        m["xT"] = asc(x[i].T)
        in_maps.append(m)
    return in_maps


_NC = None


def kernel(**inputs):
    global _NC
    if _NC is None:
        _NC = build_nc()
    in_maps = _prep_inputs(inputs)
    res = run_bass_kernel_spmd(_NC, in_maps, core_ids=list(range(8)))
    out = np.empty((8, S, D), dtype=np.float32)
    for i in range(8):
        oT = np.asarray(res.results[i]["outT"])
        out[i] = oT.reshape(D, S).T
    return out


if __name__ == "__main__":
    nc = build_nc()
    print("instructions:", nc.n_instructions())
